# version 31
# baseline (speedup 1.0000x reference)
import numpy as np
import ml_dtypes
import concourse.bass as bass
import concourse.mybir as mybir
from concourse.bass_utils import run_bass_kernel_spmd
from contextlib import ExitStack

F32 = mybir.dt.float32
BF16 = mybir.dt.bfloat16
AF = mybir.ActivationFunctionType
ALU = mybir.AluOpType

ENGINES = ("sync", "scalar", "vector", "tensor", "gpsimd")
HEADS = 16
D = 1024
NCH = 8
EPS_LN = 1e-5
EPS_B = 1e-6
THRESH = 0.7


class Prog:

    def __init__(self, nc, es):
        self.nc = nc
        self.es = es
        self.ops = {e: [] for e in ENGINES}
        self.sems = {}
        self.cnt = {}
        for name in ("sy", "sc", "ve", "te", "gp", "cc1", "cc2"):
            self.sems[name] = es.enter_context(nc.semaphore(name))
            self.cnt[name] = 0
        self.esem = {"sync": "sy", "scalar": "sc", "vector": "ve",
                     "tensor": "te", "gpsimd": "gp"}
        self.bars = []

    def add(self, engine, emit, waits=(), dma=False, dsem=None, tag=""):
        ws = []
        for w in waits:
            if w is None:
                continue
            s, c = w
            assert c <= self.cnt[s], f"wait-on-future {s}:{c}>{self.cnt[s]} @{tag}"
            if c > 0:
                ws.append((s, c))
        if dma:
            assert dsem is not None, f"dma op without dsem @{tag}"
            if dsem not in self.sems:
                self.sems[dsem] = self.es.enter_context(self.nc.semaphore(dsem))
                self.cnt[dsem] = 0
            sem, k = dsem, 16
        else:
            sem, k = self.esem[engine], 1
        self.cnt[sem] += k
        self.ops[engine].append((ws, emit, sem, k))
        return (sem, self.cnt[sem])

    def add_cc(self, emit, waits, ccname):
        ws = [w for w in waits if w is not None]
        for s, c in ws:
            assert c <= self.cnt[s]
        self.ops["gpsimd"].append((ws, emit, ccname, 1))
        self.cnt[ccname] = 1
        return (ccname, 1)

    def barrier(self):
        self.bars.append({e: len(self.ops[e]) for e in ENGINES})

    def emit(self):
        nc = self.nc
        cuts = self.bars + [{e: len(self.ops[e]) for e in ENGINES}]
        start = {e: 0 for e in ENGINES}
        seen = {e: {} for e in ENGINES}
        for cut in cuts:
            with nc.Block(no_gpsimd_drain=True) as block:
                for e in ENGINES:
                    seg = self.ops[e][start[e]:cut[e]]
                    start[e] = cut[e]
                    if not seg:
                        continue

                    def section(eng, _ops=tuple(seg), _seen=seen[e]):
                        for ws, emitfn, sem, k in _ops:
                            wm = {}
                            for (s, c) in ws:
                                wm[s] = max(wm.get(s, 0), c)
                            for s, c in wm.items():
                                if _seen.get(s, 0) >= c:
                                    continue
                                eng.wait_ge(self.sems[s], c)
                                _seen[s] = c
                            inst = emitfn(eng)
                            inst.then_inc(self.sems[sem], k)
                    getattr(block, e)(section)


class Rot:

    def __init__(self, n):
        self.n = n
        self.free = [[] for _ in range(n)]

    def fr(self, i):
        return list(self.free[i % self.n])

    def setf(self, i, arts):
        self.free[i % self.n] = [a for a in (arts if isinstance(arts, list) else [arts]) if a]


class Bank:

    def __init__(self, ap):
        self.ap = ap
        self.free = []

    def setf(self, arts):
        self.free = [a for a in (arts if isinstance(arts, list) else [arts]) if a]


def build_nc(NT, nseq, debug=False):
    T = NT * 128
    NS = NT // 4
    nc = bass.Bass("TRN2", target_bir_lowering=False,
                   dynamic_dma_scratch_size=4096)
    es = ExitStack()
    dbg_out = {}
    if debug:
        for k in ("d1", "d2", "d3"):
            dbg_out[k] = nc.dram_tensor(k, [NT * 128, D], BF16,
                                        kind="ExternalOutput").ap()
        dbg_out["dkv"] = nc.dram_tensor("dkv", [128, 1024], F32,
                                        kind="ExternalOutput").ap()
        dbg_out["dqf"] = nc.dram_tensor("dqf", [128, NCH, 512], BF16,
                                        kind="ExternalOutput").ap()
        dbg_out["dkb"] = nc.dram_tensor("dkb", [128, 512], F32,
                                        kind="ExternalOutput").ap()
        dbg_out["drp"] = nc.dram_tensor("drp", [128, 512], BF16,
                                        kind="ExternalOutput").ap()
        dbg_out["dep"] = nc.dram_tensor("dep", [128, 512], BF16,
                                        kind="ExternalOutput").ap()
        dbg_out["dmn"] = nc.dram_tensor("dmn", [128, 512], BF16,
                                        kind="ExternalOutput").ap()
        dbg_out["dq0"] = nc.dram_tensor("dq0", [128, 512], BF16,
                                        kind="ExternalOutput").ap()
        for kk in ("dqA", "dqB", "dqC", "dqD", "dga0", "dgt0"):
            dbg_out[kk] = nc.dram_tensor(kk, [128, 512], BF16,
                                         kind="ExternalOutput").ap()
        dbg_out["dkvbd"] = nc.dram_tensor("dkvbd", [128, 1024], BF16,
                                          kind="ExternalOutput").ap()
        dbg_out["dps"] = nc.dram_tensor("dps", [128, 512], F32,
                                        kind="ExternalOutput").ap()
        dbg_out["dga"] = nc.dram_tensor("dga", [128, NCH, 512], BF16,
                                        kind="ExternalOutput").ap()

    x_in = nc.dram_tensor("x", [T, D], F32, kind="ExternalInput").ap()
    w_hbm = {}
    for name in ("wq_a", "wk_a", "wv_a", "wg_a", "wo_a",
                 "wq_b", "wk_b", "wv_b", "wo_b", "wffn"):
        w_hbm[name] = nc.dram_tensor(name, [D, D], F32, kind="ExternalInput").ap()
    biasrows = nc.dram_tensor("biasrows", [7, D], F32, kind="ExternalInput").ap()
    auxf_in = nc.dram_tensor("auxf", [128, 192], F32, kind="ExternalInput").ap()
    auxb_in = nc.dram_tensor("auxb", [128, 1408], BF16, kind="ExternalInput").ap()
    idec_in = nc.dram_tensor("idec16", [16, T], F32, kind="ExternalInput").ap()
    y_out = nc.dram_tensor("y", [T, D], F32, kind="ExternalOutput").ap()
    cca_i = nc.dram_tensor("cca_i", [128, 1024], F32).ap()
    cca_o = nc.dram_tensor("cca_o", [128, 1024], F32).ap()
    ccb_i = nc.dram_tensor("ccb_i", [128, 1032], F32).ap()
    ccb_o = nc.dram_tensor("ccb_o", [128, 1032], F32).ap()

    import os as _os
    _dump_alloc = _os.environ.get("KDUMPALLOC")

    def sb(name, shape, dt):
        h = es.enter_context(nc.sbuf_tensor(name, shape, dt))
        if _dump_alloc:
            m = nc.lookup_mloc(h)
            import math
            per = math.prod(shape[1:]) * mybir.dt.size(dt)
            print(f"SB {name:8s} addr={m.addr:7d} size={per:6d} end={m.addr + per}")
        return h
    X = sb("X", [128, NT, D], BF16)
    w_sl = [sb(f"wsl{j}", [128, NCH, D], BF16) for j in range(4)]
    bB = {n: sb("bB_" + n, [128, D], BF16)
          for n in ("bk_a", "bv_a", "bo_a", "bk_b", "bv_b", "bo_b", "fb")}
    auxfS = sb("auxfS", [128, 192], F32)
    auxbS = sb("auxbS", [128, 1408], BF16)
    idecS = sb("idecS", [16, 512], F32)
    kvS = sb("kvS", [128, 1024], F32)
    ksS = sb("ksS", [128, 8], F32)
    kvBD = sb("kvBD", [128, 1024], BF16)
    dsc = sb("dsc", [1, 8], F32)
    deltaB = sb("deltaB", [128, 1], F32)
    ndeltaB = sb("ndeltaB", [128, 1], F32)
    epscol = sb("epscol", [128, 1], F32)
    absacc = sb("absacc", [128, 8], F32)
    rec16 = sb("rec16", [128, 512], BF16)
    dens = sb("dens", [16, 512], F32)
    xland = sb("xland", [128, D], F32)
    wland = sb("wland", [128, 2, D], F32)
    xln = sb("xln", [128, 2, D], BF16)
    xlnt = sb("xlnt", [128, NCH, 512], BF16)
    bn6 = sb("bn6", [128, 2, 12], F32)
    mv = sb("mv", [128, 2, 2], F32)
    nmean = sb("nmean", [128, 2], F32)
    rstd = sb("rstd", [128, 2], F32)
    rstd2 = sb("rstd2", [128, 2], F32)
    ep = sb("ep", [128, 2, 512], BF16)
    rp = sb("rp", [128, 2, 512], BF16)
    kf = sb("kf", [128, 2, D], BF16)
    vf = sb("vf", [128, 2, 8, 129], BF16)
    qfT = sb("qfT", [128, NCH, 512], BF16)
    gateC = sb("gateC", [128, 8, 512], BF16)
    recB = sb("recB", [128, 2, 512], BF16)
    t1 = sb("t1", [128, D], BF16)
    t2 = sb("t2", [128, D], BF16)
    dpsS = sb("dpsS", [128, 512], F32) if debug == 2 else None

    pT = [es.enter_context(nc.psum_tensor(f"pT{j}", [128, NCH, 128], BF16))
          for j in range(2)]
    pb = {n: Bank(es.enter_context(nc.psum_tensor("p" + n, [128, 512], F32)))
          for n in ("A", "B", "C", "D", "E", "F")}
    pTb = [Bank(pT[0]), Bank(pT[1])]
    pTb[0].ap = pT[0]
    pTb[1].ap = pT[1]

    P = Prog(nc, es)

    ident = auxbS[:, 0:128]
    onesb = auxbS[0:1, 128:256]
    e0sel = auxbS[:, 1280:1408]
    onesf_col = auxfS[:, 57:58]
    onesf_row = auxfS[0:1, 58:186]
    alpha_col = auxfS[:, 56:57]

    r_xland = Rot(1)
    r_wland = Rot(2)
    r_xln = Rot(2)
    r_bn6 = Rot(2)
    r_mv = Rot(2)
    r_cols = Rot(2)
    r_pT = Rot(2)
    r_xlnt = Rot(4)
    r_kb = Rot(1)
    r_ep = Rot(2)
    r_rp = Rot(2)
    r_tmp = Rot(2)
    r_kf = Rot(2)
    r_vf = Rot(2)
    r_gate = Rot(8)
    r_recB = Rot(2)
    r_t1 = Rot(1)
    r_t2 = Rot(1)
    r_idec = Rot(1)
    r_browb = Rot(1)

    Xready = [None] * NT
    art_aux = None
    art_eps = None

    wready = {}
    wfree = {j: [] for j in range(4)}

    def mm_group(bankap, lhs_fn, rhs_fn, n=NCH, biasrow=None):
        def emit(eng, _b=bankap, _l=lhs_fn, _r=rhs_fn, _n=n, _br=biasrow):
            inst = None
            if _br is not None:
                inst = eng.matmul(_b, e0sel, _br, start=True, stop=False)
            for k in range(_n):
                inst = eng.matmul(_b, _l(k), _r(k), start=(k == 0 and _br is None),
                                  stop=(k == _n - 1))
            return inst
        return emit

    def frontend_a(ph, i, alpha=False):
        s2 = i % 2
        a_st = P.add("vector", lambda eng, _i=i, _s=s2: (
            eng.bn_stats(out=bn6[:, _s, 0:6], in_=X[:, _i, 0:512]),
            eng.bn_stats(out=bn6[:, _s, 6:12], in_=X[:, _i, 512:1024]))[-1],
            waits=r_bn6.fr(i) + [Xready[i]], tag=f"{ph}.bnst{i}")
        a_ag = P.add("vector", lambda eng, _s=s2: eng.bn_aggr(
            out=mv[:, _s, :], in_=bn6[:, _s, :]),
            waits=[a_st] + r_mv.fr(i), tag=f"{ph}.bnag{i}")
        r_bn6.setf(i, a_ag)

        a_r1 = P.add("scalar", lambda eng, _s=s2: (
            eng.activation(out=nmean[:, _s:_s + 1], in_=mv[:, _s, 0:1],
                           func=AF.Copy, scale=-1.0),
            eng.activation(out=rstd[:, _s:_s + 1], in_=mv[:, _s, 1:2],
                           func=AF.Ln, bias=epscol[:, :]))[-1],
            waits=[a_ag, art_eps] + r_cols.fr(i), tag=f"{ph}.rstd1_{i}")
        a_rs = P.add("scalar", lambda eng, _s=s2: eng.activation(
            out=rstd[:, _s:_s + 1], in_=rstd[:, _s:_s + 1], func=AF.Exp,
            scale=-0.5), waits=[a_r1], tag=f"{ph}.rstd2_{i}")
        if alpha:
            a_rs = P.add("scalar", lambda eng, _s=s2: eng.activation(
                out=rstd2[:, _s:_s + 1], in_=rstd[:, _s:_s + 1],
                func=AF.Copy, scale=alpha_col), waits=[a_rs],
                tag=f"{ph}.rstd3_{i}")
        r_mv.setf(i, a_rs)
        rcol = rstd2 if alpha else rstd
        a_xln = P.add("gpsimd", lambda eng, _i=i, _s=s2, _r=rcol: eng.tensor_scalar(
            out=xln[:, _s, :], in0=X[:, _i, :], scalar1=nmean[:, _s:_s + 1],
            scalar2=_r[:, _s:_s + 1], op0=ALU.add, op1=ALU.mult),
            waits=[a_rs] + r_xln.fr(i), tag=f"{ph}.norm{i}")
        r_cols.setf(i, a_xln)
        return a_xln

    def frontend_b(ph, i, q, a_xln):
        s2 = i % 2
        tb = pTb[i % 2]
        def em_tr(eng, _s=s2, _tb=tb):
            inst = None
            for c in range(NCH):
                inst = eng.transpose(_tb.ap[:, c, :], xln[:, _s, c * 128:(c + 1) * 128],
                                     ident)
            return inst
        a_tr = P.add("tensor", em_tr, waits=[a_xln, art_aux] + tb.free,
                     tag=f"{ph}.tr{i}")
        r_xln.setf(i, a_tr)
        a_xlt = P.add("scalar", lambda eng, _tb=tb, _q=q: eng.copy(
            out=xlnt[:, :, _q * 128:(_q + 1) * 128], in_=_tb.ap[:, :, :]),
            waits=[a_tr] + r_xlnt.fr(q), tag=f"{ph}.xev{i}")
        tb.setf(a_xlt)
        return a_xlt

    def frontend(ph, i, q, alpha=False):
        return frontend_b(ph, i, q, frontend_a(ph, i, alpha))

    def wload_plain(ph, name, slot, c):
        if c % 2 == 1:
            return
        a_d = P.add("sync", lambda eng, _n=name, _c=c: eng.dma_start(
            out=wland[:, :, :],
            in_=w_hbm[_n][_c * 128:(_c + 2) * 128, :].rearrange(
                "(t p) d -> p t d", p=128)),
            waits=r_wland.fr(0) + r_wland.fr(1), dma=True, dsem="dW0",
            tag=f"{ph}.wd{name}{c}")
        a_c0 = P.add("scalar", lambda eng, _s=slot, _c=c: eng.copy(
            out=w_sl[_s][:, _c, :], in_=wland[:, 0, :]),
            waits=[a_d] + wfree[slot], tag=f"{ph}.wc{name}{c}")
        a_c1 = P.add("vector", lambda eng, _s=slot, _c=c: eng.tensor_copy(
            out=w_sl[_s][:, _c + 1, :], in_=wland[:, 1, :]),
            waits=[a_d] + wfree[slot], tag=f"{ph}.wc{name}{c + 1}")
        r_wland.setf(0, a_c0)
        r_wland.setf(1, a_c1)
        wready.setdefault(name, [None] * 8)[c] = a_c0
        wready[name][c + 1] = a_c1

    def wload_transposed(ph, name, slot, c):
        r = c % 2
        a_d = P.add("sync", lambda eng, _n=name, _c=c, _r=r: eng.dma_start(
            out=wland[:, _r, :], in_=w_hbm[_n][_c * 128:(_c + 1) * 128, :]),
            waits=r_wland.fr(c), dma=True, dsem=f"dW{c % 2}", tag=f"{ph}.twd{name}{c}")
        a_c = P.add("scalar", lambda eng, _r=r: eng.copy(out=t2[:, :], in_=wland[:, _r, :]),
                    waits=[a_d] + r_t2.fr(0), tag=f"{ph}.twc{name}{c}")
        r_wland.setf(c, a_c)
        tb = pTb[c % 2]
        def em_tr(eng, _tb=tb):
            inst = None
            for b in range(NCH):
                inst = eng.transpose(_tb.ap[:, b, :], t2[:, b * 128:(b + 1) * 128], ident)
            return inst
        a_tr = P.add("tensor", em_tr, waits=[a_c, art_aux] + tb.free,
                     tag=f"{ph}.ttr{name}{c}")
        r_t2.setf(0, a_tr)
        a_ev = P.add("scalar", lambda eng, _s=slot, _c=c, _tb=tb: eng.copy(
            out=w_sl[_s][:, :, _c * 128:(_c + 1) * 128], in_=_tb.ap[:, :, :]),
            waits=[a_tr] + wfree[slot], tag=f"{ph}.tte{name}{c}")
        tb.setf(a_ev)
        wready.setdefault(name, [None] * 8)[c] = a_ev

    def tern_chunk(ph, c):
        r = c % 2
        a_d = P.add("sync", lambda eng, _c=c, _r=r: eng.dma_start(
            out=wland[:, _r, :], in_=w_hbm["wffn"][_c * 128:(_c + 1) * 128, :]),
            waits=r_wland.fr(c), dma=True, dsem=f"dW{c % 2}", tag=f"{ph}.td{c}")
        a_1 = P.add("vector", lambda eng, _r=r: eng.tensor_scalar(
            out=t1[:, :], in0=wland[:, _r, :], scalar1=deltaB[:, :], scalar2=None,
            op0=ALU.is_gt), waits=[a_d, art_deltaB[0]] + r_t1.fr(0),
            tag=f"{ph}.t1{c}")
        a_2 = P.add("vector", lambda eng, _r=r: eng.tensor_scalar(
            out=t2[:, :], in0=wland[:, _r, :], scalar1=ndeltaB[:, :], scalar2=None,
            op0=ALU.is_lt), waits=[a_d, art_deltaB[1]] + r_t2.fr(0),
            tag=f"{ph}.t2{c}")
        r_wland.setf(c, a_2)
        a_tc = P.add("vector", lambda eng: eng.tensor_tensor(
            out=t1[:, :], in0=t1[:, :], in1=t2[:, :], op=ALU.subtract),
            waits=[a_2], tag=f"{ph}.tsub{c}")
        r_t2.setf(0, a_tc)
        tb = pTb[c % 2]
        def em_tr(eng, _tb=tb):
            inst = None
            for b in range(NCH):
                inst = eng.transpose(_tb.ap[:, b, :], t1[:, b * 128:(b + 1) * 128], ident)
            return inst
        a_tr = P.add("tensor", em_tr, waits=[a_tc, art_aux] + tb.free,
                     tag=f"{ph}.ttr{c}")
        r_t1.setf(0, a_tr)
        a_ev = P.add("scalar", lambda eng, _c=c, _tb=tb: eng.copy(
            out=w_sl[1][:, :, _c * 128:(_c + 1) * 128], in_=_tb.ap[:, :, :]),
            waits=[a_tr] + wfree[1], tag=f"{ph}.tev{c}")
        tb.setf(a_ev)
        wready.setdefault("ternT", [None] * 8)[c] = a_ev

    art_aux_f = P.add("sync", lambda eng: eng.dma_start(out=auxfS[:, :], in_=auxf_in),
                      dma=True, dsem="dAf", tag="ph0.auxf")
    art_aux_b = P.add("sync", lambda eng: eng.dma_start(out=auxbS[:, :], in_=auxb_in),
                      dma=True, dsem="dAb", tag="ph0.auxb")
    art_aux = art_aux_b
    art_eps = P.add("gpsimd", lambda eng: eng.memset(epscol[:, :], EPS_LN),
                    tag="ph0.eps")
    art_vf_ones = P.add("gpsimd", lambda eng: eng.memset(vf[:, :, :, 128:129], 1.0),
                        tag="ph0.vfones")
    art_r16z = P.add("gpsimd", lambda eng: eng.memset(rec16[:, :], 0.0),
                     tag="ph0.r16z")

    xring = {"j": 0}

    def load_x_pair(i):
        for t in (i, i + 1):
            s = xring["j"] % 3
            xring["j"] += 1
            if s < 2:
                land_ap = wland[:, s, :]
                frees, track, slot = r_wland.fr(s), r_wland, s
            else:
                land_ap = xland[:, :]
                frees, track, slot = r_xland.fr(0), r_xland, 0
            a_d = P.add("sync", lambda eng, _t=t, _ap=land_ap: eng.dma_start(
                out=_ap, in_=x_in[_t * 128:(_t + 1) * 128, :]),
                waits=frees, dma=True, dsem=f"dX{s}", tag=f"ph0.xd{t}")
            if 4 <= t < 12:
                a_c = P.add("gpsimd", lambda eng, _t=t, _ap=land_ap: eng.tensor_copy(
                    out=X[:, _t, :], in_=_ap), waits=[a_d], tag=f"ph0.xc{t}")
            elif t % 2 == 0:
                a_c = P.add("scalar", lambda eng, _t=t, _ap=land_ap: eng.copy(
                    out=X[:, _t, :], in_=_ap), waits=[a_d], tag=f"ph0.xc{t}")
            else:
                a_c = P.add("vector", lambda eng, _t=t, _ap=land_ap: eng.tensor_copy(
                    out=X[:, _t, :], in_=_ap), waits=[a_d], tag=f"ph0.xc{t}")
            track.setf(slot, a_c)
            Xready[t] = a_c

    def load_x_tile(i):
        load_x_pair(i)
    for i in range(0, 4, 2):
        load_x_pair(i)

    bnames = ("bk_a", "bv_a", "bo_a", "bk_b", "bv_b", "bo_b", "fb")
    for j, bn in enumerate(bnames):
        a_d = P.add("sync", lambda eng, _j=j: eng.dma_start(
            out=wland[0:1, _j % 2, :], in_=biasrows[_j:_j + 1, :]),
            waits=r_wland.fr(j), dma=True, dsem=f"dW{j % 2}", tag=f"ph0.bd{j}")
        a_c = P.add("vector", lambda eng, _j=j: eng.tensor_copy(
            out=xln[0:1, 0, :], in_=wland[0:1, _j % 2, :]),
            waits=[a_d] + r_browb.fr(0), tag=f"ph0.bc{j}")
        r_wland.setf(j, a_c)
        arts = []
        for h in range(2):
            bank = pb["A"] if h == 0 else pb["B"]
            a_m = P.add("tensor", lambda eng, _h=h, _b=bank: eng.matmul(
                _b.ap[:, :], onesb, xln[0:1, 0, _h * 512:(_h + 1) * 512],
                start=True, stop=True),
                waits=[a_c, art_aux_b] + bank.free, tag=f"ph0.bm{j}{h}")
            a_e = P.add("vector", lambda eng, _h=h, _b=bank, _bn=bn: eng.tensor_copy(
                out=bB[_bn][:, _h * 512:(_h + 1) * 512], in_=_b.ap[:, :]),
                waits=[a_m], tag=f"ph0.be{j}{h}")
            bank.setf(a_e)
            arts.append(a_e)
        r_browb.setf(0, arts)
        bB[bn + "_ready"] = arts[-1]
    r_xln.setf(0, list(r_browb.free[0]))

    for c in range(NCH):
        wload_plain("ph0", "wk_a", 0, c)
    for c in range(NCH):
        wload_plain("ph0", "wv_a", 1, c)
    for i in range(4, min(12, NT), 2):
        load_x_pair(i)

    art_deltaB = [None, None]

    def phase_p1(ph, is_b, kslot, vslot, kvbanks, side_work, proj=None):
        kname = "wk_b" if is_b else "wk_a"
        vname = "wv_b" if is_b else "wv_a"
        bkB = bB["bk_b" if is_b else "bk_a"]
        bvB = bB["bv_b" if is_b else "bv_a"]
        bkB_rdy = bB[("bk_b" if is_b else "bk_a") + "_ready"]
        bvB_rdy = bB[("bv_b" if is_b else "bv_a") + "_ready"]
        proj = proj or [pb["A"], pb["B"], pb["C"]]
        pi = 0
        art_kv_last = None
        if is_b:
            pair_off = [(0, 0), (0, 129), (0, 258), (1, 0), (1, 129), (1, 258),
                        (2, 0), (2, 129)]
            first_in_bank = (0, 3, 6)
        else:
            pair_off = [(0, 0), (0, 128), (0, 256), (0, 384), (1, 0), (1, 128),
                        (1, 256), (1, 384)]
            first_in_bank = (0, 4)

        fe_art = [None] * (NT + 2)
        fe_a = [None] * (NT + 2)
        fe_art[0] = frontend(ph, 0, 0, alpha=False)
        if NT > 1:
            fe_art[1] = frontend(ph, 1, 1, alpha=False)
        pend_kv = None
        for i in range(NT):
            if i + 2 < NT:
                fe_a[i + 2] = frontend_a(ph, i + 2, alpha=False)
            a_xlt = fe_art[i]
            art_kf = [None, None]
            art_vf = [None, None]
            proj_arts = []
            for h in range(2):
                bank = proj[pi % len(proj)]
                pi += 1
                a_mm = P.add("tensor", mm_group(
                    bank.ap[:, :],
                    lambda k, _i=i: xlnt[:, k, (_i % 4) * 128:((_i % 4) + 1) * 128],
                    lambda k, _h=h, _s=kslot: w_sl[_s][:, k, _h * 512:(_h + 1) * 512],
                    biasrow=bkB[:, h * 512:(h + 1) * 512]),
                    waits=[a_xlt, bkB_rdy, art_aux_b] + wready[kname] + bank.free,
                    tag=f"{ph}.kmm{i}{h}")
                proj_arts.append(a_mm)
                if is_b:
                    a_kf = P.add("vector", lambda eng, _h=h, _i=i, _b=bank:
                        eng.tensor_scalar(
                            out=kf[:, _i % 2, _h * 512:(_h + 1) * 512],
                            in0=_b.ap[:, :], scalar1=auxfS[:, _i:_i + 1],
                            scalar2=0.0, op0=ALU.mult, op1=ALU.max),
                        waits=[a_mm, art_aux_f] + r_kf.fr(i), tag=f"{ph}.kf{i}{h}")
                    bank.setf(a_kf)
                else:
                    a_m = P.add("vector", lambda eng, _h=h, _b=bank: eng.tensor_scalar(
                        out=ep[:, _h, :], in0=_b.ap[:, :], scalar1=0.0, scalar2=None,
                        op0=ALU.min), waits=[a_mm] + r_ep.fr(h), tag=f"{ph}.km{i}{h}")
                    a_r = P.add("vector", lambda eng, _h=h, _b=bank: eng.tensor_scalar(
                        out=rp[:, _h, :], in0=_b.ap[:, :], scalar1=0.0, scalar2=None,
                        op0=ALU.max), waits=[a_mm] + r_rp.fr(h), tag=f"{ph}.kr{i}{h}")
                    bank.setf([a_r])
                    a_e = P.add("scalar", lambda eng, _h=h: eng.activation(
                        out=ep[:, _h, :], in_=ep[:, _h, :], func=AF.Exp),
                        waits=[a_m], tag=f"{ph}.ke{i}{h}")
                    a_kf = P.add("gpsimd", lambda eng, _h=h, _i=i: eng.tensor_tensor(
                        out=kf[:, _i % 2, _h * 512:(_h + 1) * 512], in0=rp[:, _h, :],
                        in1=ep[:, _h, :], op=ALU.add),
                        waits=[a_e, a_r] + r_kf.fr(i), tag=f"{ph}.kf{i}{h}")
                    r_rp.setf(h, a_kf)
                    r_ep.setf(h, a_kf)
                art_kf[h] = a_kf
            if pend_kv is not None:
                pend_kv()
                pend_kv = None
            if i + 2 < NT:
                fe_art[i + 2] = frontend_b(ph, i + 2, (i + 2) % 4, fe_a[i + 2])
            for h in range(2):
                bank = proj[pi % len(proj)]
                pi += 1
                a_mm = P.add("tensor", mm_group(
                    bank.ap[:, :],
                    lambda k, _i=i: xlnt[:, k, (_i % 4) * 128:((_i % 4) + 1) * 128],
                    lambda k, _h=h, _s=vslot: w_sl[_s][:, k, _h * 512:(_h + 1) * 512]),
                    waits=[a_xlt] + wready[vname] + bank.free,
                    tag=f"{ph}.vmm{i}{h}")
                proj_arts.append(a_mm)
                a_v = P.add("vector", lambda eng, _h=h, _i=i, _b=bank: eng.tensor_tensor(
                    out=vf[:, _i % 2, 4 * _h:4 * _h + 4, 0:128], in0=_b.ap[:, :],
                    in1=bvB[:, _h * 512:(_h + 1) * 512], op=ALU.add),
                    waits=[a_mm, bvB_rdy] + r_vf.fr(i), tag=f"{ph}.vf{i}{h}")
                bank.setf(a_v)
                art_vf[h] = a_v
            r_xlnt.setf(i % 4, proj_arts[-1])

            def mk_kv(_i, _kf, _vf):
                def emit_kv():
                    nonlocal art_kv_last
                    ncol = 129 if is_b else 128
                    def em_kv(eng, _i=_i, _po=pair_off, _nc=ncol,
                              _fib=first_in_bank):
                        inst = None
                        for p in range(8):
                            bidx, off = _po[p]
                            inst = eng.matmul(
                                kvbanks[bidx].ap[:, off:off + _nc],
                                kf[:, _i % 2, p * 128:(p + 1) * 128],
                                vf[:, _i % 2, p, 0:_nc],
                                start=(_i == 0 and p in _fib),
                                stop=(_i == NT - 1),
                                skip_group_check=True)
                        return inst
                    kvw = [_kf[0], _kf[1], _vf[0], _vf[1]]
                    if _i == 0:
                        for b in kvbanks:
                            kvw += b.free
                        kvw.append(art_vf_ones)
                    a_kv = P.add("tensor", em_kv, waits=kvw, tag=f"{ph}.kv{_i}")
                    r_kf.setf(_i, a_kv)
                    r_vf.setf(_i, a_kv)
                    art_kv_last = a_kv
                return emit_kv
            pend_kv = mk_kv(i, art_kf, art_vf)
            if side_work:
                side_work.pop(0)(i)
        pend_kv()
        while side_work:
            side_work.pop(0)(NT - 1)
        evs = []
        for p in range(8):
            bidx, off = pair_off[p]
            def em_ev(eng, _p=p, _b=kvbanks[bidx], _o=off, _isb=is_b):
                inst = eng.copy(out=kvS[:, _p * 128:(_p + 1) * 128],
                                in_=_b.ap[:, _o:_o + 128])
                if _isb:
                    inst = eng.copy(out=ksS[:, _p:_p + 1], in_=_b.ap[:, _o + 128:_o + 129])
                return inst
            a_e = P.add("scalar", em_ev, waits=[art_kv_last], tag=f"{ph}.kvev{p}")
            evs.append(a_e)
        for b in kvbanks:
            b.setf(evs[-1])
        return evs[-1], proj_arts

    def exchange_start(ph, is_b, art_kvS, ccname, cci, cco):
        a_d1 = P.add("sync", lambda eng: eng.dma_start(
            out=cci[:, 0:1024], in_=kvS[:, :]), waits=[art_kvS], dma=True,
            dsem="dG", tag=f"{ph}.d1")
        if is_b:
            a_d1 = P.add("sync", lambda eng: eng.dma_start(
                out=cci[:, 1024:1032], in_=ksS[:, :]), waits=[art_kvS], dma=True,
                dsem="dG", tag=f"{ph}.d1b")
        a_cc = P.add_cc(lambda eng: eng.collective_compute(
            "AllReduce", ALU.add,
            replica_groups=[[0, 1], [2, 3], [4, 5], [6, 7]],
            ins=[cci], outs=[cco]), waits=[a_d1], ccname=ccname)
        return a_cc, cco

    def exchange_finish(ph, is_b, a_cc, cco, kvBD_free):
        a_d2 = P.add("sync", lambda eng: eng.dma_start(
            out=kvS[:, :], in_=cco[:, 0:1024]), waits=[a_cc], dma=True,
            dsem="dG", tag=f"{ph}.d2")
        a_d2b = None
        if is_b:
            a_d2b = P.add("sync", lambda eng: eng.dma_start(
                out=ksS[:, :], in_=cco[:, 1024:1032]), waits=[a_cc], dma=True,
                dsem="dG", tag=f"{ph}.d2b")
        a_z = P.add("gpsimd", lambda eng: eng.memset(kvBD[:, :], 0.0),
                    waits=kvBD_free, tag=f"{ph}.z")
        a_q = None
        for p in range(8):
            def em_q(eng, _p=p):
                eng.tensor_copy(out=kvBD[0:64, _p * 128:_p * 128 + 64],
                                in_=kvS[0:64, _p * 128:_p * 128 + 64])
                return eng.tensor_copy(out=kvBD[64:128, _p * 128 + 64:(_p + 1) * 128],
                                       in_=kvS[64:128, _p * 128 + 64:(_p + 1) * 128])
            a_q = P.add("vector", em_q, waits=[a_d2, a_d2b, a_z], tag=f"{ph}.q{p}")
        if is_b:
            a_zz = P.add("gpsimd", lambda eng: eng.memset(t1[:, :], 0.0),
                         waits=r_t1.fr(0), tag=f"{ph}.zz")
            for c in range(8):
                def em_ks(eng, _c=c):
                    eng.tensor_copy(out=t1[0:64, _c * 128 + 2 * _c:_c * 128 + 2 * _c + 1],
                                    in_=ksS[0:64, _c:_c + 1])
                    return eng.tensor_copy(
                        out=t1[64:128, _c * 128 + 2 * _c + 1:_c * 128 + 2 * _c + 2],
                        in_=ksS[64:128, _c:_c + 1])
                a_q = P.add("vector", em_ks, waits=[a_d2b, a_zz], tag=f"{ph}.ks{c}")
        return a_q

    def phase_p2(ph, is_b, qslot, oslot, kvbd_ready, side_work, fe_pre=None):
        qname = "wq_b" if is_b else "wq_a"
        oname = "wo_b" if is_b else "wo_a"
        boB = bB["bo_b" if is_b else "bo_a"]
        boB_rdy = bB[("bo_b" if is_b else "bo_a") + "_ready"]
        qbanks = [pb["A"], pb["B"]]
        gbanks = [pb["C"], pb["D"]]
        abanks = [pb["E"], pb["F"]]
        dbanks = [pb["C"], pb["D"]]
        rbanks = [pb["A"], pb["B"]]
        obanks = [pb["E"], pb["F"]]
        qcol0 = 48 if is_b else 32
        prev_omm = None
        fe_arts = (fe_pre if fe_pre is not None
                   else [frontend(ph, u, u, alpha=False) for u in range(4)])
        feA_next = [None, None]
        for s in range(NS):
            art_qf = [None] * 8
            art_ga = [None] * 8
            qg_arts = []
            if is_b:
                a_id = P.add("sync", lambda eng, _s=s: eng.dma_start(
                    out=idecS[:, :], in_=idec_in[:, _s * 512:(_s + 1) * 512]),
                    waits=r_idec.fr(0), dma=True, dsem="dI", tag=f"{ph}.id{s}")
            for c in range(8):
                qbank = qbanks[c % 2]
                a_qmm = P.add("tensor", mm_group(
                    qbank.ap[:, :],
                    lambda k, _c=c, _q=qslot: w_sl[_q][:, k, _c * 128:(_c + 1) * 128],
                    lambda k: xlnt[:, k, :]),
                    waits=fe_arts + wready[qname] + qbank.free,
                    tag=f"{ph}.qmm{s}{c}")
                qg_arts.append(a_qmm)
                if is_b:
                    a_qf = P.add("vector", lambda eng, _c=c, _b=qbank: eng.tensor_scalar(
                        out=qfT[:, _c, :], in0=_b.ap[:, :],
                        scalar1=auxfS[:, qcol0 + _c:qcol0 + _c + 1], scalar2=0.0,
                        op0=ALU.add, op1=ALU.max),
                        waits=[a_qmm, art_aux_f, prev_omm], tag=f"{ph}.qf{s}{c}")
                    qbank.setf(a_qf)
                else:
                    a_m = P.add("vector", lambda eng, _c=c, _b=qbank: eng.tensor_scalar(
                        out=ep[:, _c % 2, :], in0=_b.ap[:, :],
                        scalar1=auxfS[:, qcol0 + _c:qcol0 + _c + 1], scalar2=0.0,
                        op0=ALU.add, op1=ALU.min),
                        waits=[a_qmm, art_aux_f] + r_ep.fr(c), tag=f"{ph}.qm{s}{c}")
                    a_r = P.add("vector", lambda eng, _c=c, _b=qbank: eng.tensor_scalar(
                        out=rp[:, _c % 2, :], in0=_b.ap[:, :],
                        scalar1=auxfS[:, qcol0 + _c:qcol0 + _c + 1], scalar2=0.0,
                        op0=ALU.add, op1=ALU.max),
                        waits=[a_qmm, art_aux_f] + r_rp.fr(c),
                        tag=f"{ph}.qr{s}{c}")
                    qbank.setf([a_r])
                    a_dmp = None
                    if debug == 2 and s == 0 and c == 0 and not is_b:
                        a_dmp = P.add("sync", lambda eng: eng.dma_start(
                            out=dbg_out["dmn"], in_=ep[:, 0, :]),
                            waits=[a_m], dma=True, dsem="dDmp", tag=f"{ph}.dmn")
                        a_dmp = P.add("sync", lambda eng: eng.dma_start(
                            out=dbg_out["drp"], in_=rp[:, 0, :]),
                            waits=[a_r, a_dmp], dma=True, dsem="dDmp", tag=f"{ph}.drp")
                    a_e = P.add("scalar", lambda eng, _c=c: eng.activation(
                        out=ep[:, _c % 2, :], in_=ep[:, _c % 2, :], func=AF.Exp),
                        waits=[a_m, a_dmp], tag=f"{ph}.qe{s}{c}")
                    if debug == 2 and s == 0 and c == 0 and not is_b:
                        a_dmp = P.add("sync", lambda eng: eng.dma_start(
                            out=dbg_out["dep"], in_=ep[:, 0, :]),
                            waits=[a_e], dma=True, dsem="dDmp", tag=f"{ph}.dep")
                    a_qf = P.add("gpsimd", lambda eng, _c=c: eng.tensor_tensor(
                        out=qfT[:, _c, :], in0=rp[:, _c % 2, :], in1=ep[:, _c % 2, :],
                        op=ALU.add), waits=[a_e, a_r, prev_omm, a_dmp],
                        tag=f"{ph}.qf{s}{c}")
                    if debug == 2 and s == 0 and c == 0 and not is_b:
                        a_qf = P.add("sync", lambda eng: eng.dma_start(
                            out=dbg_out["dq0"], in_=qfT[:, 0, :]),
                            waits=[a_qf], dma=True, dsem="dDq", tag=f"{ph}.dq0")
                    r_rp.setf(c, a_qf)
                    r_ep.setf(c, a_qf)
                art_qf[c] = a_qf
                if debug == 2 and s == 0 and not is_b and c in (1, 3, 5, 7):
                    kk = {1: "dqA", 3: "dqB", 5: "dqC", 7: "dqD"}[c]
                    art_qf[c] = P.add("sync", lambda eng, _k=kk: eng.dma_start(
                        out=dbg_out[_k], in_=qfT[:, 0, :]),
                        waits=[a_qf], dma=True, dsem="dDq", tag=f"{ph}.{kk}")
            if not is_b:
                for c in range(8):
                    gbank = gbanks[c % 2]
                    a_gmm = P.add("tensor", mm_group(
                        gbank.ap[:, :],
                        lambda k, _c=c: w_sl[3][:, k, _c * 128:(_c + 1) * 128],
                        lambda k: xlnt[:, k, :]),
                        waits=wready["wg_a"] + gbank.free, tag=f"{ph}.gmm{s}{c}")
                    qg_arts.append(a_gmm)
                    a_g = P.add("scalar", lambda eng, _c=c, _b=gbank: eng.activation(
                        out=gateC[:, _c, :], in_=_b.ap[:, :], func=AF.Sigmoid,
                        bias=auxfS[:, 40 + _c:40 + _c + 1]),
                        waits=[a_gmm, art_aux_f] + r_gate.fr(c), tag=f"{ph}.g{s}{c}")
                    gbank.setf(a_g)
                    art_ga[c] = a_g
            a_dump_qf = None
            if is_b:
                dbank = dbanks[s % 2]
                a_dmm = P.add("tensor", mm_group(
                    dbank.ap[:, :],
                    lambda k: t1[:, k * 128:(k + 1) * 128],
                    lambda k: qfT[:, k, :]),
                    waits=[art_qf[7], kvbd_ready] + dbank.free, tag=f"{ph}.dmm{s}")
                a_ds = P.add("vector", lambda eng, _b=dbank: eng.tensor_tensor(
                    out=dens[0:16, :], in0=_b.ap[0:16, :], in1=idecS[:, :], op=ALU.add),
                    waits=[a_dmm, a_id], tag=f"{ph}.ds{s}")
                dbank.setf(a_ds)
                r_idec.setf(0, a_ds)
                a_rec = P.add("vector", lambda eng: eng.reciprocal(
                    out=dens[0:16, :], in_=dens[0:16, :]), waits=[a_ds],
                    tag=f"{ph}.rec{s}")
                a_rec = P.add("vector", lambda eng: eng.tensor_copy(
                    out=rec16[0:16, :], in_=dens[0:16, :]), waits=[a_rec],
                    tag=f"{ph}.rec16{s}")
            for c in range(8):
                if is_b:
                    rbank = rbanks[c % 2]
                    a_rbm = P.add("tensor", lambda eng, _c=c, _b=rbank: eng.matmul(
                        _b.ap[:, :], auxbS[:, 256 + _c * 128:256 + (_c + 1) * 128],
                        rec16[:, :], start=True, stop=True),
                        waits=[a_rec, art_aux_b, art_r16z] + rbank.free,
                        tag=f"{ph}.rbm{s}{c}")
                    a_rbe = P.add("vector", lambda eng, _c=c, _b=rbank: eng.tensor_copy(
                        out=recB[:, _c % 2, :], in_=_b.ap[:, :]),
                        waits=[a_rbm] + r_recB.fr(c), tag=f"{ph}.rbe{s}{c}")
                    rbank.setf(a_rbe)
                abank = abanks[c % 2]
                a_amm = P.add("tensor", lambda eng, _c=c, _b=abank: eng.matmul(
                    _b.ap[:, :], kvBD[:, _c * 128:(_c + 1) * 128], qfT[:, _c, :],
                    start=True, stop=True),
                    waits=[art_qf[c], kvbd_ready] + abank.free,
                    tag=f"{ph}.amm{s}{c}")
                if is_b:
                    a_ga = P.add("vector", lambda eng, _c=c, _b=abank: eng.tensor_tensor(
                        out=qfT[:, _c, :], in0=_b.ap[:, :], in1=recB[:, _c % 2, :],
                        op=ALU.mult), waits=[a_amm, a_rbe], tag=f"{ph}.ga{s}{c}")
                    r_recB.setf(c, a_ga)
                else:
                    a_ga = P.add("vector", lambda eng, _c=c, _b=abank: eng.tensor_tensor(
                        out=gateC[:, _c, :], in0=gateC[:, _c, :], in1=_b.ap[:, :],
                        op=ALU.mult), waits=[a_amm, art_ga[c]], tag=f"{ph}.ga{s}{c}")
                abank.setf(a_ga)
                art_ga[c] = a_ga
            if s + 1 < NS:
                feA_next = [frontend_a(ph, 4 * (s + 1) + u, alpha=False)
                            for u in range(2)]
            a_dump_ga = None
            for u in range(4):
                i = 4 * s + u
                for h in range(2):
                    obank = obanks[(2 * u + h) % 2]
                    ga_src = qfT if is_b else gateC
                    a_omm = P.add("tensor", mm_group(
                        obank.ap[:, :],
                        lambda k, _u=u, _g=ga_src: _g[:, k, _u * 128:(_u + 1) * 128],
                        lambda k, _h=h, _o=oslot: w_sl[_o][:, k, _h * 512:(_h + 1) * 512],
                        biasrow=boB[:, h * 512:(h + 1) * 512]),
                        waits=[art_ga[7], a_dump_ga, boB_rdy, art_aux_b]
                              + wready[oname] + obank.free,
                        tag=f"{ph}.omm{s}{u}{h}")
                    prev_omm = a_omm
                    a_x = P.add("vector", lambda eng, _i=i, _h=h, _b=obank:
                        eng.tensor_tensor(
                            out=X[:, _i, _h * 512:(_h + 1) * 512],
                            in0=X[:, _i, _h * 512:(_h + 1) * 512], in1=_b.ap[:, :],
                            op=ALU.add),
                        waits=[a_omm], tag=f"{ph}.ox{s}{u}{h}")
                    obank.setf(a_x)
                    Xready[i] = a_x
            for u in range(4):
                r_xlnt.setf(u, qg_arts[-1])
            if not is_b:
                for c2 in range(8):
                    r_gate.setf(c2, prev_omm)
            if s + 1 < NS:
                b0 = frontend_b(ph, 4 * (s + 1), 0, feA_next[0])
                b1 = frontend_b(ph, 4 * (s + 1) + 1, 1, feA_next[1])
                a2 = frontend_a(ph, 4 * (s + 1) + 2, alpha=False)
                a3 = frontend_a(ph, 4 * (s + 1) + 3, alpha=False)
                b2 = frontend_b(ph, 4 * (s + 1) + 2, 2, a2)
                b3 = frontend_b(ph, 4 * (s + 1) + 3, 3, a3)
                fe_arts = [b0, b1, b2, b3]
            if side_work:
                side_work.pop(0)(s)
        while side_work:
            side_work.pop(0)(NS - 1)
        return

    def phase_ffn(ph, final, side_work):
        banks = [pb["A"], pb["B"], pb["C"]]
        pi = 0
        fe_art = [None] * (NT + 2)
        fe_a = [None] * (NT + 2)
        fe_art[0] = frontend(ph, 0, 0, alpha=True)
        if NT > 1:
            fe_art[1] = frontend(ph, 1, 1, alpha=True)
        for i in range(NT):
            if i + 2 < NT:
                fe_a[i + 2] = frontend_a(ph, i + 2, alpha=True)
            a_xlt = fe_art[i]
            last_arts = []
            a_o_last = None
            mm_arts = []
            evac_work = []
            for h in range(2):
                bank = banks[pi % 3]
                pi += 1
                a_mm = P.add("tensor", mm_group(
                    bank.ap[:, :],
                    lambda k, _i=i: xlnt[:, k, (_i % 4) * 128:((_i % 4) + 1) * 128],
                    lambda k, _h=h: w_sl[1][:, k, _h * 512:(_h + 1) * 512],
                    biasrow=bB["fb"][:, h * 512:(h + 1) * 512]),
                    waits=[a_xlt, bB["fb_ready"], art_aux_b] + wready["ternT"]
                          + bank.free,
                    tag=f"{ph}.fmm{i}{h}")
                last_arts.append(a_mm)
                mm_arts.append((h, bank, a_mm))
            r_xlnt.setf(i % 4, last_arts[-1])
            if i + 2 < NT:
                fe_art[i + 2] = frontend_b(ph, i + 2, (i + 2) % 4, fe_a[i + 2])
            for h, bank, a_mm in mm_arts:
                if final:
                    a_o = P.add("vector", lambda eng, _i=i, _h=h, _b=bank:
                        eng.tensor_tensor(
                            out=xland[:, _h * 512:(_h + 1) * 512],
                            in0=X[:, _i, _h * 512:(_h + 1) * 512], in1=_b.ap[:, :],
                            op=ALU.add),
                        waits=[a_mm] + r_xland.fr(i), tag=f"{ph}.fo{i}{h}")
                    bank.setf(a_o)
                    a_o_last = a_o
                else:
                    a_x = P.add("vector", lambda eng, _i=i, _h=h, _b=bank:
                        eng.tensor_tensor(
                            out=X[:, _i, _h * 512:(_h + 1) * 512],
                            in0=X[:, _i, _h * 512:(_h + 1) * 512], in1=_b.ap[:, :],
                            op=ALU.add),
                        waits=[a_mm], tag=f"{ph}.fx{i}{h}")
                    bank.setf(a_x)
                    Xready[i] = a_x
            if final:
                a_st = P.add("sync", lambda eng, _i=i: eng.dma_start(
                    out=y_out[_i * 128:(_i + 1) * 128, :], in_=xland[:, :]),
                    waits=[a_o_last], dma=True, dsem="dO", tag=f"{ph}.st{i}")
                r_xland.setf(i, a_st)
            if side_work:
                side_work.pop(0)(i)
        while side_work:
            side_work.pop(0)(NT - 1)
        return

    side_tasks = []
    for i in range(12, NT, 2):
        side_tasks.append(lambda i2, _i=i: load_x_pair(_i))
    for c in range(NCH):
        side_tasks.append(lambda i, _c=c: wload_plain("ph1", "wq_a", 2, _c))
    for c in range(NCH):
        side_tasks.append(lambda i, _c=c: wload_plain("ph1", "wg_a", 3, _c))

    def wmean_chunk(c):
        r = c % 2
        a_d = P.add("sync", lambda eng, _c=c, _r=r: eng.dma_start(
            out=wland[:, _r, :], in_=w_hbm["wffn"][_c * 128:(_c + 1) * 128, :]),
            waits=r_wland.fr(c), dma=True, dsem=f"dW{c % 2}", tag=f"ph1.md{c}")
        a_a = P.add("vector", lambda eng, _c=c, _r=r: eng.tensor_reduce(
            out=absacc[:, _c:_c + 1], in_=wland[:, _r, :], axis=mybir.AxisListType.X,
            op=ALU.add, apply_absolute_value=True),
            waits=[a_d], tag=f"ph1.ma{c}")
        r_wland.setf(c, a_a)
        wready.setdefault("wmean", [None] * 8)[c] = a_a
    for c in range(NCH):
        side_tasks.append(lambda i, _c=c: wmean_chunk(_c))
    side1 = []
    ntasks = len(side_tasks)
    pos = 0
    for t in range(NT):
        take = (ntasks * (t + 1)) // NT - (ntasks * t) // NT
        chunk = side_tasks[pos:pos + take]
        pos += take
        side1.append(lambda i, _ch=tuple(chunk): [f(i) for f in _ch])

    kvb_a = [pb["D"], pb["E"]]
    art_kvS_a, _ = phase_p1("ph1", False, 0, 1, kvb_a, side1,
                            proj=[pb["A"], pb["B"], pb["C"], pb["F"]])
    wfree[0] = []
    wfree[0] = [art_kvS_a]
    wfree[1] = [art_kvS_a]

    a_dm1 = P.add("tensor", lambda eng: eng.matmul(
        pb["F"].ap[0:1, 0:8], onesf_col, absacc[:, :], start=True, stop=True),
        waits=wready["wmean"] + [art_aux_f] + pb["F"].free, tag="ph1.dm1")
    a_dr = P.add("vector", lambda eng: eng.tensor_reduce(
        out=dsc[0:1, 0:1], in_=pb["F"].ap[0:1, 0:8], axis=mybir.AxisListType.X,
        op=ALU.add), waits=[a_dm1], tag="ph1.dr")
    a_dsc = P.add("vector", lambda eng: eng.tensor_scalar(
        out=dsc[0:1, 1:2], in0=dsc[0:1, 0:1], scalar1=THRESH / float(D * D),
        scalar2=None, op0=ALU.mult), waits=[a_dr], tag="ph1.dsc")
    a_dbm = P.add("tensor", lambda eng: eng.matmul(
        pb["F"].ap[:, 8:9], onesf_row, dsc[0:1, 1:2], start=True, stop=True),
        waits=[a_dsc], tag="ph1.dbm")
    a_db = P.add("vector", lambda eng: eng.tensor_copy(
        out=deltaB[:, :], in_=pb["F"].ap[:, 8:9]), waits=[a_dbm], tag="ph1.db")
    a_ndb = P.add("vector", lambda eng: eng.tensor_scalar(
        out=ndeltaB[:, :], in0=deltaB[:, :], scalar1=-1.0, scalar2=None,
        op0=ALU.mult), waits=[a_db], tag="ph1.ndb")
    pb["F"].setf(a_db)
    art_deltaB[0] = a_db
    art_deltaB[1] = a_ndb

    P.barrier()
    xa_cc, xa_cco = exchange_start("phxa", False, art_kvS_a, "cc1", cca_i, cca_o)
    for c in range(NCH):
        wload_plain("ph2", "wo_a", 0, c)
    fe_pre2 = [frontend("ph2", u, u, alpha=False) for u in range(4)]
    kvbd_a = exchange_finish("phxa", False, xa_cc, xa_cco, [])
    if debug:
        P.add("sync", lambda eng: eng.dma_start(out=dbg_out["dkv"], in_=kvS[:, :]),
              waits=[kvbd_a], dma=True, dsem="dDkv", tag="dbg.dkv")
        kvbd_a = P.add("sync", lambda eng: eng.dma_start(
            out=dbg_out["dkvbd"], in_=kvBD[:, :]), waits=[kvbd_a], dma=True,
            dsem="dDkv", tag="dbg.dkvbd")

    side2 = [lambda s, _c=c: tern_chunk("ph2", _c) for c in range(NCH)]
    while len(side2) < NS:
        side2.append(lambda s: None)
    P.barrier()
    phase_p2("ph2", False, 2, 0, kvbd_a, side2, fe_pre=fe_pre2)
    wfree[2] = [Xready[NT - 1]]
    wfree[3] = [Xready[NT - 1]]
    wfree[0] = [Xready[NT - 1]]

    def dump_X(key):
        if not debug:
            return
        for i in range(NT):
            P.add("gpsimd", lambda eng, _i=i, _k=key: eng.dma_start(
                out=dbg_out[_k][_i * 128:(_i + 1) * 128, :], in_=X[:, _i, :]),
                waits=[Xready[i]], dma=True, dsem="dDX", tag=f"dbg{key}{i}")

    dump_X("d1")

    side3 = []
    for c in range(NCH):
        side3.append(lambda i, _c=c: wload_transposed("ph3", "wk_b", 2, _c))
    for c in range(NCH):
        side3.append(lambda i, _c=c: wload_transposed("ph3", "wv_b", 3, _c))
    while len(side3) < NT:
        side3.append(lambda i: None)
    P.barrier()
    phase_ffn("ph3", False, side3)

    dump_X("d2")

    side4 = [lambda i, _c=c: wload_transposed("ph4", "wq_b", 0, _c)
             for c in range(NCH)]
    while len(side4) < NT:
        side4.append(lambda i: None)
    P.barrier()
    kvb_b = [pb["D"], pb["E"], pb["F"]]
    art_kvS_b, _ = phase_p1("ph4", True, 2, 3, kvb_b, side4)
    wfree[2] = [art_kvS_b]
    wfree[3] = [art_kvS_b]

    P.barrier()
    xb_cc, xb_cco = exchange_start("phxb", True, art_kvS_b, "cc2", ccb_i, ccb_o)
    for c in range(NCH):
        wload_transposed("ph5", "wo_b", 3, c)
    fe_pre5 = [frontend("ph5", u, u, alpha=False) for u in range(4)]
    kvbd_b = exchange_finish("phxb", True, xb_cc, xb_cco, [Xready[NT - 1]])

    P.barrier()
    phase_p2("ph5", True, 0, 3, kvbd_b, [], fe_pre=fe_pre5)

    dump_X("d3")

    P.barrier()
    phase_ffn("ph6", True, [])

    fin_waits = [("dO", P.cnt.get("dO", 0))]
    if debug:
        fin_waits.append(("dDX", P.cnt.get("dDX", 0)))
    P.add("sync", lambda eng: eng.nop(),
          waits=[w for w in fin_waits if w[1] > 0], tag="fin")

    P.emit()
    es.close()
    return nc


def _prep_in_maps(inp, NT, nseq):
    T = NT * 128
    n_cores = 8
    x = np.ascontiguousarray(
        np.asarray(inp["L0_attn_a_x"], dtype=np.float32).reshape(-1, D))
    assert x.shape[0] == n_cores * T

    wmap = {
        "wq_a": inp["L0_attn_a_Wq"], "wk_a": inp["L0_attn_a_Wk"],
        "wv_a": inp["L0_attn_a_Wv"], "wg_a": inp["L0_attn_a_Wg"],
        "wo_a": inp["L0_attn_a_Wo"],
        "wq_b": inp["L0_attn_b_W_q"], "wk_b": inp["L0_attn_b_W_k"],
        "wv_b": inp["L0_attn_b_W_v"], "wo_b": inp["L0_attn_b_W_o"],
        "wffn": inp["L0_ffn_weight"],
    }
    wmap = {k: np.ascontiguousarray(np.asarray(v, dtype=np.float32))
            for k, v in wmap.items()}
    biasrows = np.ascontiguousarray(np.stack([
        np.asarray(inp["L0_attn_a_bk"], np.float32),
        np.asarray(inp["L0_attn_a_bv"], np.float32),
        np.asarray(inp["L0_attn_a_bo"], np.float32),
        np.asarray(inp["L0_attn_b_b_k"], np.float32),
        np.asarray(inp["L0_attn_b_b_v"], np.float32),
        np.asarray(inp["L0_attn_b_b_o"], np.float32),
        np.asarray(inp["L0_ffn_bias"], np.float32)]))
    bq_a = np.asarray(inp["L0_attn_a_bq"], np.float32)
    bg_a = np.asarray(inp["L0_attn_a_bg"], np.float32)
    bq_b = np.asarray(inp["L0_attn_b_b_q"], np.float32)
    alpha = np.asarray(inp["L0_ffn_alpha"], np.float32).reshape(-1)[0]

    sel = np.zeros((16, 1024), dtype=np.float32)
    for c in range(8):
        sel[2 * c, c * 128:c * 128 + 64] = 1.0
        sel[2 * c + 1, c * 128 + 64:(c + 1) * 128] = 1.0
    auxb = np.zeros((128, 1408), dtype=np.float32)
    auxb[:, 0:128] = np.eye(128, dtype=np.float32)
    auxb[:, 128:256] = 1.0
    auxb[0:16, 256:1280] = sel
    auxb[0, 1280:1408] = 1.0
    auxb = auxb.astype(ml_dtypes.bfloat16)

    in_maps = []
    for cid in range(n_cores):
        s0 = (cid % 2) * T
        n = s0 + np.arange(T, dtype=np.float64)
        dec = np.exp(-n / float(nseq)).astype(np.float32)
        idec = (EPS_B * np.exp(n / float(nseq))).astype(np.float32)
        auxf = np.zeros((128, 192), dtype=np.float32)
        auxf[:, 0:NT] = np.ascontiguousarray(dec.reshape(NT, 128).T)
        auxf[:, 32:40] = bq_a.reshape(8, 128).T
        auxf[:, 40:48] = bg_a.reshape(8, 128).T
        auxf[:, 48:56] = bq_b.reshape(8, 128).T
        auxf[:, 56] = alpha
        auxf[:, 57] = 1.0
        auxf[:, 58:186] = 1.0
        m = {"x": np.ascontiguousarray(x[cid * T:(cid + 1) * T]),
             "biasrows": biasrows,
             "auxf": np.ascontiguousarray(auxf),
             "auxb": np.ascontiguousarray(auxb),
             "idec16": np.ascontiguousarray(np.tile(idec, (16, 1)))}
        m.update(wmap)
        in_maps.append(m)
    return in_maps


def kernel(**inputs):
    NT = 32
    nseq = 8192
    in_maps = _prep_in_maps(inputs, NT, nseq)
    nc = build_nc(NT, nseq)
    res = run_bass_kernel_spmd(nc, in_maps, core_ids=list(range(8)))
    y = np.concatenate([np.asarray(res.results[c]["y"]) for c in range(8)], axis=0)
    return y.reshape(4, 8192, D).astype(np.float32)


# version 32
# speedup vs baseline: 1.0039x; 1.0039x over previous
import numpy as np
import ml_dtypes
import concourse.bass as bass
import concourse.mybir as mybir
from concourse.bass_utils import run_bass_kernel_spmd
from contextlib import ExitStack

F32 = mybir.dt.float32
BF16 = mybir.dt.bfloat16
AF = mybir.ActivationFunctionType
ALU = mybir.AluOpType

ENGINES = ("sync", "scalar", "vector", "tensor", "gpsimd")
HEADS = 16
D = 1024
NCH = 8
EPS_LN = 1e-5
EPS_B = 1e-6
THRESH = 0.7


class Prog:

    def __init__(self, nc, es):
        self.nc = nc
        self.es = es
        self.ops = {e: [] for e in ENGINES}
        self.sems = {}
        self.cnt = {}
        for name in ("sy", "sc", "ve", "te", "gp", "cc1", "cc2"):
            self.sems[name] = es.enter_context(nc.semaphore(name))
            self.cnt[name] = 0
        self.esem = {"sync": "sy", "scalar": "sc", "vector": "ve",
                     "tensor": "te", "gpsimd": "gp"}
        self.bars = []

    def add(self, engine, emit, waits=(), dma=False, dsem=None, tag=""):
        ws = []
        for w in waits:
            if w is None:
                continue
            s, c = w
            assert c <= self.cnt[s], f"wait-on-future {s}:{c}>{self.cnt[s]} @{tag}"
            if c > 0:
                ws.append((s, c))
        if dma:
            assert dsem is not None, f"dma op without dsem @{tag}"
            if dsem not in self.sems:
                self.sems[dsem] = self.es.enter_context(self.nc.semaphore(dsem))
                self.cnt[dsem] = 0
            sem, k = dsem, 16
        else:
            sem, k = self.esem[engine], 1
        self.cnt[sem] += k
        self.ops[engine].append((ws, emit, sem, k))
        return (sem, self.cnt[sem])

    def add_cc(self, emit, waits, ccname):
        ws = [w for w in waits if w is not None]
        for s, c in ws:
            assert c <= self.cnt[s]
        self.ops["gpsimd"].append((ws, emit, ccname, 1))
        self.cnt[ccname] = 1
        return (ccname, 1)

    def barrier(self):
        self.bars.append({e: len(self.ops[e]) for e in ENGINES})

    def emit(self):
        nc = self.nc
        cuts = self.bars + [{e: len(self.ops[e]) for e in ENGINES}]
        start = {e: 0 for e in ENGINES}
        seen = {e: {} for e in ENGINES}
        for cut in cuts:
            with nc.Block(no_gpsimd_drain=True) as block:
                for e in ENGINES:
                    seg = self.ops[e][start[e]:cut[e]]
                    start[e] = cut[e]
                    if not seg:
                        continue

                    def section(eng, _ops=tuple(seg), _seen=seen[e]):
                        for ws, emitfn, sem, k in _ops:
                            wm = {}
                            for (s, c) in ws:
                                wm[s] = max(wm.get(s, 0), c)
                            for s, c in wm.items():
                                if _seen.get(s, 0) >= c:
                                    continue
                                eng.wait_ge(self.sems[s], c)
                                _seen[s] = c
                            inst = emitfn(eng)
                            inst.then_inc(self.sems[sem], k)
                    getattr(block, e)(section)


class Rot:

    def __init__(self, n):
        self.n = n
        self.free = [[] for _ in range(n)]

    def fr(self, i):
        return list(self.free[i % self.n])

    def setf(self, i, arts):
        self.free[i % self.n] = [a for a in (arts if isinstance(arts, list) else [arts]) if a]


class Bank:

    def __init__(self, ap):
        self.ap = ap
        self.free = []

    def setf(self, arts):
        self.free = [a for a in (arts if isinstance(arts, list) else [arts]) if a]


def build_nc(NT, nseq, debug=False):
    T = NT * 128
    NS = NT // 4
    nc = bass.Bass("TRN2", target_bir_lowering=False,
                   dynamic_dma_scratch_size=4096)
    es = ExitStack()
    dbg_out = {}
    if debug:
        for k in ("d1", "d2", "d3"):
            dbg_out[k] = nc.dram_tensor(k, [NT * 128, D], BF16,
                                        kind="ExternalOutput").ap()
        dbg_out["dkv"] = nc.dram_tensor("dkv", [128, 1024], F32,
                                        kind="ExternalOutput").ap()
        dbg_out["dqf"] = nc.dram_tensor("dqf", [128, NCH, 512], BF16,
                                        kind="ExternalOutput").ap()
        dbg_out["dkb"] = nc.dram_tensor("dkb", [128, 512], F32,
                                        kind="ExternalOutput").ap()
        dbg_out["drp"] = nc.dram_tensor("drp", [128, 512], BF16,
                                        kind="ExternalOutput").ap()
        dbg_out["dep"] = nc.dram_tensor("dep", [128, 512], BF16,
                                        kind="ExternalOutput").ap()
        dbg_out["dmn"] = nc.dram_tensor("dmn", [128, 512], BF16,
                                        kind="ExternalOutput").ap()
        dbg_out["dq0"] = nc.dram_tensor("dq0", [128, 512], BF16,
                                        kind="ExternalOutput").ap()
        for kk in ("dqA", "dqB", "dqC", "dqD", "dga0", "dgt0"):
            dbg_out[kk] = nc.dram_tensor(kk, [128, 512], BF16,
                                         kind="ExternalOutput").ap()
        dbg_out["dkvbd"] = nc.dram_tensor("dkvbd", [128, 1024], BF16,
                                          kind="ExternalOutput").ap()
        dbg_out["dps"] = nc.dram_tensor("dps", [128, 512], F32,
                                        kind="ExternalOutput").ap()
        dbg_out["dga"] = nc.dram_tensor("dga", [128, NCH, 512], BF16,
                                        kind="ExternalOutput").ap()

    x_in = nc.dram_tensor("x", [T, D], F32, kind="ExternalInput").ap()
    w_hbm = {}
    for name in ("wq_a", "wk_a", "wv_a", "wg_a", "wo_a",
                 "wq_b", "wk_b", "wv_b", "wo_b", "wffn"):
        w_hbm[name] = nc.dram_tensor(name, [D, D], F32, kind="ExternalInput").ap()
    biasrows = nc.dram_tensor("biasrows", [7, D], F32, kind="ExternalInput").ap()
    auxf_in = nc.dram_tensor("auxf", [128, 192], F32, kind="ExternalInput").ap()
    auxb_in = nc.dram_tensor("auxb", [128, 1408], BF16, kind="ExternalInput").ap()
    idec_in = nc.dram_tensor("idec16", [16, T], F32, kind="ExternalInput").ap()
    y_out = nc.dram_tensor("y", [T, D], F32, kind="ExternalOutput").ap()
    cca_i = nc.dram_tensor("cca_i", [128, 1024], F32).ap()
    cca_o = nc.dram_tensor("cca_o", [128, 1024], F32).ap()
    ccb_i = nc.dram_tensor("ccb_i", [128, 1032], F32).ap()
    ccb_o = nc.dram_tensor("ccb_o", [128, 1032], F32).ap()

    import os as _os
    _dump_alloc = _os.environ.get("KDUMPALLOC")

    def sb(name, shape, dt):
        h = es.enter_context(nc.sbuf_tensor(name, shape, dt))
        if _dump_alloc:
            m = nc.lookup_mloc(h)
            import math
            per = math.prod(shape[1:]) * mybir.dt.size(dt)
            print(f"SB {name:8s} addr={m.addr:7d} size={per:6d} end={m.addr + per}")
        return h
    X = sb("X", [128, NT, D], BF16)
    w_sl = [sb(f"wsl{j}", [128, NCH, D], BF16) for j in range(4)]
    bB = {n: sb("bB_" + n, [128, D], BF16)
          for n in ("bk_a", "bv_a", "bo_a", "bk_b", "bv_b", "bo_b", "fb")}
    auxfS = sb("auxfS", [128, 192], F32)
    auxbS = sb("auxbS", [128, 1408], BF16)
    idecS = sb("idecS", [16, 512], F32)
    kvS = sb("kvS", [128, 1024], F32)
    ksS = sb("ksS", [128, 8], F32)
    kvBD = sb("kvBD", [128, 1024], BF16)
    dsc = sb("dsc", [1, 8], F32)
    deltaB = sb("deltaB", [128, 1], F32)
    ndeltaB = sb("ndeltaB", [128, 1], F32)
    epscol = sb("epscol", [128, 1], F32)
    absacc = sb("absacc", [128, 8], F32)
    rec16 = sb("rec16", [128, 512], BF16)
    dens = sb("dens", [16, 512], F32)
    xland = sb("xland", [128, D], F32)
    wland = sb("wland", [128, 2, D], F32)
    xln = sb("xln", [128, 2, D], BF16)
    xlnt = sb("xlnt", [128, NCH, 512], BF16)
    bn6 = sb("bn6", [128, 2, 12], F32)
    mv = sb("mv", [128, 2, 2], F32)
    nmean = sb("nmean", [128, 2], F32)
    rstd = sb("rstd", [128, 2], F32)
    rstd2 = sb("rstd2", [128, 2], F32)
    ep = sb("ep", [128, 2, 512], BF16)
    rp = sb("rp", [128, 2, 512], BF16)
    kf = sb("kf", [128, 2, D], BF16)
    vf = sb("vf", [128, 2, 8, 129], BF16)
    qfT = sb("qfT", [128, NCH, 512], BF16)
    gateC = sb("gateC", [128, 8, 512], BF16)
    recB = sb("recB", [128, 2, 512], BF16)
    t1 = sb("t1", [128, D], BF16)
    t2 = sb("t2", [128, D], BF16)
    dpsS = sb("dpsS", [128, 512], F32) if debug == 2 else None

    pT = [es.enter_context(nc.psum_tensor(f"pT{j}", [128, NCH, 128], BF16))
          for j in range(2)]
    pb = {n: Bank(es.enter_context(nc.psum_tensor("p" + n, [128, 512], F32)))
          for n in ("A", "B", "C", "D", "E", "F")}
    pTb = [Bank(pT[0]), Bank(pT[1])]
    pTb[0].ap = pT[0]
    pTb[1].ap = pT[1]

    P = Prog(nc, es)

    ident = auxbS[:, 0:128]
    onesb = auxbS[0:1, 128:256]
    e0sel = auxbS[:, 1280:1408]
    onesf_col = auxfS[:, 57:58]
    onesf_row = auxfS[0:1, 58:186]
    alpha_col = auxfS[:, 56:57]

    r_xland = Rot(1)
    r_wland = Rot(2)
    r_xln = Rot(2)
    r_bn6 = Rot(2)
    r_mv = Rot(2)
    r_cols = Rot(2)
    r_pT = Rot(2)
    r_xlnt = Rot(4)
    r_kb = Rot(1)
    r_ep = Rot(2)
    r_rp = Rot(2)
    r_tmp = Rot(2)
    r_kf = Rot(2)
    r_vf = Rot(2)
    r_gate = Rot(8)
    r_recB = Rot(2)
    r_t1 = Rot(1)
    r_t2 = Rot(1)
    r_idec = Rot(1)
    r_browb = Rot(1)

    Xready = [None] * NT
    art_aux = None
    art_eps = None

    wready = {}
    wfree = {j: [] for j in range(4)}

    def mm_group(bankap, lhs_fn, rhs_fn, n=NCH, biasrow=None):
        def emit(eng, _b=bankap, _l=lhs_fn, _r=rhs_fn, _n=n, _br=biasrow):
            inst = None
            if _br is not None:
                inst = eng.matmul(_b, e0sel, _br, start=True, stop=False)
            for k in range(_n):
                inst = eng.matmul(_b, _l(k), _r(k), start=(k == 0 and _br is None),
                                  stop=(k == _n - 1))
            return inst
        return emit

    def frontend_a(ph, i, alpha=False):
        s2 = i % 2
        a_st = P.add("vector", lambda eng, _i=i, _s=s2: (
            eng.bn_stats(out=bn6[:, _s, 0:6], in_=X[:, _i, 0:512]),
            eng.bn_stats(out=bn6[:, _s, 6:12], in_=X[:, _i, 512:1024]))[-1],
            waits=r_bn6.fr(i) + [Xready[i]], tag=f"{ph}.bnst{i}")
        a_ag = P.add("vector", lambda eng, _s=s2: eng.bn_aggr(
            out=mv[:, _s, :], in_=bn6[:, _s, :]),
            waits=[a_st] + r_mv.fr(i), tag=f"{ph}.bnag{i}")
        r_bn6.setf(i, a_ag)

        a_r1 = P.add("scalar", lambda eng, _s=s2: (
            eng.activation(out=nmean[:, _s:_s + 1], in_=mv[:, _s, 0:1],
                           func=AF.Copy, scale=-1.0),
            eng.activation(out=rstd[:, _s:_s + 1], in_=mv[:, _s, 1:2],
                           func=AF.Ln, bias=epscol[:, :]))[-1],
            waits=[a_ag, art_eps] + r_cols.fr(i), tag=f"{ph}.rstd1_{i}")
        a_rs = P.add("scalar", lambda eng, _s=s2: eng.activation(
            out=rstd[:, _s:_s + 1], in_=rstd[:, _s:_s + 1], func=AF.Exp,
            scale=-0.5), waits=[a_r1], tag=f"{ph}.rstd2_{i}")
        if alpha:
            a_rs = P.add("scalar", lambda eng, _s=s2: eng.activation(
                out=rstd2[:, _s:_s + 1], in_=rstd[:, _s:_s + 1],
                func=AF.Copy, scale=alpha_col), waits=[a_rs],
                tag=f"{ph}.rstd3_{i}")
        r_mv.setf(i, a_rs)
        rcol = rstd2 if alpha else rstd
        a_xln = P.add("gpsimd", lambda eng, _i=i, _s=s2, _r=rcol: eng.tensor_scalar(
            out=xln[:, _s, :], in0=X[:, _i, :], scalar1=nmean[:, _s:_s + 1],
            scalar2=_r[:, _s:_s + 1], op0=ALU.add, op1=ALU.mult),
            waits=[a_rs] + r_xln.fr(i), tag=f"{ph}.norm{i}")
        r_cols.setf(i, a_xln)
        return a_xln

    def frontend_b(ph, i, q, a_xln):
        s2 = i % 2
        tb = pTb[i % 2]
        def em_tr(eng, _s=s2, _tb=tb):
            inst = None
            for c in range(NCH):
                inst = eng.transpose(_tb.ap[:, c, :], xln[:, _s, c * 128:(c + 1) * 128],
                                     ident)
            return inst
        a_tr = P.add("tensor", em_tr, waits=[a_xln, art_aux] + tb.free,
                     tag=f"{ph}.tr{i}")
        r_xln.setf(i, a_tr)
        a_xlt = P.add("scalar", lambda eng, _tb=tb, _q=q: eng.copy(
            out=xlnt[:, :, _q * 128:(_q + 1) * 128], in_=_tb.ap[:, :, :]),
            waits=[a_tr] + r_xlnt.fr(q), tag=f"{ph}.xev{i}")
        tb.setf(a_xlt)
        return a_xlt

    def frontend(ph, i, q, alpha=False):
        return frontend_b(ph, i, q, frontend_a(ph, i, alpha))

    def wload_plain(ph, name, slot, c):
        if c % 2 == 1:
            return
        a_d = P.add("sync", lambda eng, _n=name, _c=c: eng.dma_start(
            out=wland[:, :, :],
            in_=w_hbm[_n][_c * 128:(_c + 2) * 128, :].rearrange(
                "(t p) d -> p t d", p=128)),
            waits=r_wland.fr(0) + r_wland.fr(1), dma=True, dsem="dW0",
            tag=f"{ph}.wd{name}{c}")
        a_c0 = P.add("scalar", lambda eng, _s=slot, _c=c: eng.copy(
            out=w_sl[_s][:, _c, :], in_=wland[:, 0, :]),
            waits=[a_d] + wfree[slot], tag=f"{ph}.wc{name}{c}")
        a_c1 = P.add("vector", lambda eng, _s=slot, _c=c: eng.tensor_copy(
            out=w_sl[_s][:, _c + 1, :], in_=wland[:, 1, :]),
            waits=[a_d] + wfree[slot], tag=f"{ph}.wc{name}{c + 1}")
        r_wland.setf(0, a_c0)
        r_wland.setf(1, a_c1)
        wready.setdefault(name, [None] * 8)[c] = a_c0
        wready[name][c + 1] = a_c1

    def wload_transposed(ph, name, slot, c):
        r = c % 2
        a_d = P.add("sync", lambda eng, _n=name, _c=c, _r=r: eng.dma_start(
            out=wland[:, _r, :], in_=w_hbm[_n][_c * 128:(_c + 1) * 128, :]),
            waits=r_wland.fr(c), dma=True, dsem=f"dW{c % 2}", tag=f"{ph}.twd{name}{c}")
        a_c = P.add("scalar", lambda eng, _r=r: eng.copy(out=t2[:, :], in_=wland[:, _r, :]),
                    waits=[a_d] + r_t2.fr(0), tag=f"{ph}.twc{name}{c}")
        r_wland.setf(c, a_c)
        tb = pTb[c % 2]
        def em_tr(eng, _tb=tb):
            inst = None
            for b in range(NCH):
                inst = eng.transpose(_tb.ap[:, b, :], t2[:, b * 128:(b + 1) * 128], ident)
            return inst
        a_tr = P.add("tensor", em_tr, waits=[a_c, art_aux] + tb.free,
                     tag=f"{ph}.ttr{name}{c}")
        r_t2.setf(0, a_tr)
        a_ev = P.add("scalar", lambda eng, _s=slot, _c=c, _tb=tb: eng.copy(
            out=w_sl[_s][:, :, _c * 128:(_c + 1) * 128], in_=_tb.ap[:, :, :]),
            waits=[a_tr] + wfree[slot], tag=f"{ph}.tte{name}{c}")
        tb.setf(a_ev)
        wready.setdefault(name, [None] * 8)[c] = a_ev

    def tern_chunk(ph, c):
        r = c % 2
        a_d = P.add("sync", lambda eng, _c=c, _r=r: eng.dma_start(
            out=wland[:, _r, :], in_=w_hbm["wffn"][_c * 128:(_c + 1) * 128, :]),
            waits=r_wland.fr(c), dma=True, dsem=f"dW{c % 2}", tag=f"{ph}.td{c}")
        a_1 = P.add("vector", lambda eng, _r=r: eng.tensor_scalar(
            out=t1[:, :], in0=wland[:, _r, :], scalar1=deltaB[:, :], scalar2=None,
            op0=ALU.is_gt), waits=[a_d, art_deltaB[0]] + r_t1.fr(0),
            tag=f"{ph}.t1{c}")
        a_2 = P.add("vector", lambda eng, _r=r: eng.tensor_scalar(
            out=t2[:, :], in0=wland[:, _r, :], scalar1=ndeltaB[:, :], scalar2=None,
            op0=ALU.is_lt), waits=[a_d, art_deltaB[1]] + r_t2.fr(0),
            tag=f"{ph}.t2{c}")
        r_wland.setf(c, a_2)
        a_tc = P.add("vector", lambda eng: eng.tensor_tensor(
            out=t1[:, :], in0=t1[:, :], in1=t2[:, :], op=ALU.subtract),
            waits=[a_2], tag=f"{ph}.tsub{c}")
        r_t2.setf(0, a_tc)
        tb = pTb[c % 2]
        def em_tr(eng, _tb=tb):
            inst = None
            for b in range(NCH):
                inst = eng.transpose(_tb.ap[:, b, :], t1[:, b * 128:(b + 1) * 128], ident)
            return inst
        a_tr = P.add("tensor", em_tr, waits=[a_tc, art_aux] + tb.free,
                     tag=f"{ph}.ttr{c}")
        r_t1.setf(0, a_tr)
        a_ev = P.add("scalar", lambda eng, _c=c, _tb=tb: eng.copy(
            out=w_sl[1][:, :, _c * 128:(_c + 1) * 128], in_=_tb.ap[:, :, :]),
            waits=[a_tr] + wfree[1], tag=f"{ph}.tev{c}")
        tb.setf(a_ev)
        wready.setdefault("ternT", [None] * 8)[c] = a_ev

    art_aux_f = P.add("sync", lambda eng: eng.dma_start(out=auxfS[:, :], in_=auxf_in),
                      dma=True, dsem="dAf", tag="ph0.auxf")
    art_aux_b = P.add("sync", lambda eng: eng.dma_start(out=auxbS[:, :], in_=auxb_in),
                      dma=True, dsem="dAb", tag="ph0.auxb")
    art_aux = art_aux_b
    art_eps = P.add("gpsimd", lambda eng: eng.memset(epscol[:, :], EPS_LN),
                    tag="ph0.eps")
    art_vf_ones = P.add("gpsimd", lambda eng: eng.memset(vf[:, :, :, 128:129], 1.0),
                        tag="ph0.vfones")
    art_r16z = P.add("gpsimd", lambda eng: eng.memset(rec16[:, :], 0.0),
                     tag="ph0.r16z")

    xring = {"j": 0}

    def load_x_pair(i):
        for t in (i, i + 1):
            s = xring["j"] % 3
            xring["j"] += 1
            if s < 2:
                land_ap = wland[:, s, :]
                frees, track, slot = r_wland.fr(s), r_wland, s
            else:
                land_ap = xland[:, :]
                frees, track, slot = r_xland.fr(0), r_xland, 0
            a_d = P.add("sync", lambda eng, _t=t, _ap=land_ap: eng.dma_start(
                out=_ap, in_=x_in[_t * 128:(_t + 1) * 128, :]),
                waits=frees, dma=True, dsem=f"dX{s}", tag=f"ph0.xd{t}")
            if t % 2 == 0:
                a_c = P.add("scalar", lambda eng, _t=t, _ap=land_ap: eng.copy(
                    out=X[:, _t, :], in_=_ap), waits=[a_d], tag=f"ph0.xc{t}")
            else:
                a_c = P.add("vector", lambda eng, _t=t, _ap=land_ap: eng.tensor_copy(
                    out=X[:, _t, :], in_=_ap), waits=[a_d], tag=f"ph0.xc{t}")
            track.setf(slot, a_c)
            Xready[t] = a_c

    def load_x_tile(i):
        load_x_pair(i)
    for i in range(0, 4, 2):
        load_x_pair(i)

    bnames = ("bk_a", "bv_a", "bo_a", "bk_b", "bv_b", "bo_b", "fb")
    for j, bn in enumerate(bnames):
        a_d = P.add("sync", lambda eng, _j=j: eng.dma_start(
            out=wland[0:1, _j % 2, :], in_=biasrows[_j:_j + 1, :]),
            waits=r_wland.fr(j), dma=True, dsem=f"dW{j % 2}", tag=f"ph0.bd{j}")
        a_c = P.add("vector", lambda eng, _j=j: eng.tensor_copy(
            out=xln[0:1, 0, :], in_=wland[0:1, _j % 2, :]),
            waits=[a_d] + r_browb.fr(0), tag=f"ph0.bc{j}")
        r_wland.setf(j, a_c)
        arts = []
        for h in range(2):
            bank = pb["A"] if h == 0 else pb["B"]
            a_m = P.add("tensor", lambda eng, _h=h, _b=bank: eng.matmul(
                _b.ap[:, :], onesb, xln[0:1, 0, _h * 512:(_h + 1) * 512],
                start=True, stop=True),
                waits=[a_c, art_aux_b] + bank.free, tag=f"ph0.bm{j}{h}")
            a_e = P.add("vector", lambda eng, _h=h, _b=bank, _bn=bn: eng.tensor_copy(
                out=bB[_bn][:, _h * 512:(_h + 1) * 512], in_=_b.ap[:, :]),
                waits=[a_m], tag=f"ph0.be{j}{h}")
            bank.setf(a_e)
            arts.append(a_e)
        r_browb.setf(0, arts)
        bB[bn + "_ready"] = arts[-1]
    r_xln.setf(0, list(r_browb.free[0]))

    for c in range(NCH):
        wload_plain("ph0", "wk_a", 0, c)
    for c in range(NCH):
        wload_plain("ph0", "wv_a", 1, c)
    for i in range(4, min(12, NT), 2):
        load_x_pair(i)

    art_deltaB = [None, None]

    def phase_p1(ph, is_b, kslot, vslot, kvbanks, side_work, proj=None):
        kname = "wk_b" if is_b else "wk_a"
        vname = "wv_b" if is_b else "wv_a"
        bkB = bB["bk_b" if is_b else "bk_a"]
        bvB = bB["bv_b" if is_b else "bv_a"]
        bkB_rdy = bB[("bk_b" if is_b else "bk_a") + "_ready"]
        bvB_rdy = bB[("bv_b" if is_b else "bv_a") + "_ready"]
        proj = proj or [pb["A"], pb["B"], pb["C"]]
        pi = 0
        art_kv_last = None
        if is_b:
            pair_off = [(0, 0), (0, 129), (0, 258), (1, 0), (1, 129), (1, 258),
                        (2, 0), (2, 129)]
            first_in_bank = (0, 3, 6)
        else:
            pair_off = [(0, 0), (0, 128), (0, 256), (0, 384), (1, 0), (1, 128),
                        (1, 256), (1, 384)]
            first_in_bank = (0, 4)

        fe_art = [None] * (NT + 2)
        fe_a = [None] * (NT + 2)
        fe_art[0] = frontend(ph, 0, 0, alpha=False)
        if NT > 1:
            fe_art[1] = frontend(ph, 1, 1, alpha=False)
        pend_kv = None
        for i in range(NT):
            if i + 2 < NT:
                fe_a[i + 2] = frontend_a(ph, i + 2, alpha=False)
            a_xlt = fe_art[i]
            art_kf = [None, None]
            art_vf = [None, None]
            proj_arts = []
            for h in range(2):
                bank = proj[pi % len(proj)]
                pi += 1
                a_mm = P.add("tensor", mm_group(
                    bank.ap[:, :],
                    lambda k, _i=i: xlnt[:, k, (_i % 4) * 128:((_i % 4) + 1) * 128],
                    lambda k, _h=h, _s=kslot: w_sl[_s][:, k, _h * 512:(_h + 1) * 512],
                    biasrow=bkB[:, h * 512:(h + 1) * 512]),
                    waits=[a_xlt, bkB_rdy, art_aux_b] + wready[kname] + bank.free,
                    tag=f"{ph}.kmm{i}{h}")
                proj_arts.append(a_mm)
                if is_b:
                    a_kf = P.add("vector", lambda eng, _h=h, _i=i, _b=bank:
                        eng.tensor_scalar(
                            out=kf[:, _i % 2, _h * 512:(_h + 1) * 512],
                            in0=_b.ap[:, :], scalar1=auxfS[:, _i:_i + 1],
                            scalar2=0.0, op0=ALU.mult, op1=ALU.max),
                        waits=[a_mm, art_aux_f] + r_kf.fr(i), tag=f"{ph}.kf{i}{h}")
                    bank.setf(a_kf)
                else:
                    a_m = P.add("vector", lambda eng, _h=h, _b=bank: eng.tensor_scalar(
                        out=ep[:, _h, :], in0=_b.ap[:, :], scalar1=0.0, scalar2=None,
                        op0=ALU.min), waits=[a_mm] + r_ep.fr(h), tag=f"{ph}.km{i}{h}")
                    a_r = P.add("vector", lambda eng, _h=h, _b=bank: eng.tensor_scalar(
                        out=rp[:, _h, :], in0=_b.ap[:, :], scalar1=0.0, scalar2=None,
                        op0=ALU.max), waits=[a_mm] + r_rp.fr(h), tag=f"{ph}.kr{i}{h}")
                    bank.setf([a_r])
                    a_e = P.add("scalar", lambda eng, _h=h: eng.activation(
                        out=ep[:, _h, :], in_=ep[:, _h, :], func=AF.Exp),
                        waits=[a_m], tag=f"{ph}.ke{i}{h}")
                    a_kf = P.add("gpsimd", lambda eng, _h=h, _i=i: eng.tensor_tensor(
                        out=kf[:, _i % 2, _h * 512:(_h + 1) * 512], in0=rp[:, _h, :],
                        in1=ep[:, _h, :], op=ALU.add),
                        waits=[a_e, a_r] + r_kf.fr(i), tag=f"{ph}.kf{i}{h}")
                    r_rp.setf(h, a_kf)
                    r_ep.setf(h, a_kf)
                art_kf[h] = a_kf
            if pend_kv is not None:
                pend_kv()
                pend_kv = None
            if i + 2 < NT:
                fe_art[i + 2] = frontend_b(ph, i + 2, (i + 2) % 4, fe_a[i + 2])
            for h in range(2):
                bank = proj[pi % len(proj)]
                pi += 1
                a_mm = P.add("tensor", mm_group(
                    bank.ap[:, :],
                    lambda k, _i=i: xlnt[:, k, (_i % 4) * 128:((_i % 4) + 1) * 128],
                    lambda k, _h=h, _s=vslot: w_sl[_s][:, k, _h * 512:(_h + 1) * 512]),
                    waits=[a_xlt] + wready[vname] + bank.free,
                    tag=f"{ph}.vmm{i}{h}")
                proj_arts.append(a_mm)
                a_v = P.add("vector", lambda eng, _h=h, _i=i, _b=bank: eng.tensor_tensor(
                    out=vf[:, _i % 2, 4 * _h:4 * _h + 4, 0:128], in0=_b.ap[:, :],
                    in1=bvB[:, _h * 512:(_h + 1) * 512], op=ALU.add),
                    waits=[a_mm, bvB_rdy] + r_vf.fr(i), tag=f"{ph}.vf{i}{h}")
                bank.setf(a_v)
                art_vf[h] = a_v
            r_xlnt.setf(i % 4, proj_arts[-1])

            def mk_kv(_i, _kf, _vf):
                def emit_kv():
                    nonlocal art_kv_last
                    ncol = 129 if is_b else 128
                    def em_kv(eng, _i=_i, _po=pair_off, _nc=ncol,
                              _fib=first_in_bank):
                        inst = None
                        for p in range(8):
                            bidx, off = _po[p]
                            inst = eng.matmul(
                                kvbanks[bidx].ap[:, off:off + _nc],
                                kf[:, _i % 2, p * 128:(p + 1) * 128],
                                vf[:, _i % 2, p, 0:_nc],
                                start=(_i == 0 and p in _fib),
                                stop=(_i == NT - 1),
                                skip_group_check=True)
                        return inst
                    kvw = [_kf[0], _kf[1], _vf[0], _vf[1]]
                    if _i == 0:
                        for b in kvbanks:
                            kvw += b.free
                        kvw.append(art_vf_ones)
                    a_kv = P.add("tensor", em_kv, waits=kvw, tag=f"{ph}.kv{_i}")
                    r_kf.setf(_i, a_kv)
                    r_vf.setf(_i, a_kv)
                    art_kv_last = a_kv
                return emit_kv
            pend_kv = mk_kv(i, art_kf, art_vf)
            if side_work:
                side_work.pop(0)(i)
        pend_kv()
        while side_work:
            side_work.pop(0)(NT - 1)
        evs = []
        for p in range(8):
            bidx, off = pair_off[p]
            def em_ev(eng, _p=p, _b=kvbanks[bidx], _o=off, _isb=is_b):
                inst = eng.copy(out=kvS[:, _p * 128:(_p + 1) * 128],
                                in_=_b.ap[:, _o:_o + 128])
                if _isb:
                    inst = eng.copy(out=ksS[:, _p:_p + 1], in_=_b.ap[:, _o + 128:_o + 129])
                return inst
            a_e = P.add("scalar", em_ev, waits=[art_kv_last], tag=f"{ph}.kvev{p}")
            evs.append(a_e)
        for b in kvbanks:
            b.setf(evs[-1])
        return evs[-1], proj_arts

    def exchange_start(ph, is_b, art_kvS, ccname, cci, cco):
        a_d1 = P.add("sync", lambda eng: eng.dma_start(
            out=cci[:, 0:1024], in_=kvS[:, :]), waits=[art_kvS], dma=True,
            dsem="dG", tag=f"{ph}.d1")
        if is_b:
            a_d1 = P.add("sync", lambda eng: eng.dma_start(
                out=cci[:, 1024:1032], in_=ksS[:, :]), waits=[art_kvS], dma=True,
                dsem="dG", tag=f"{ph}.d1b")
        a_cc = P.add_cc(lambda eng: eng.collective_compute(
            "AllReduce", ALU.add,
            replica_groups=[[0, 1], [2, 3], [4, 5], [6, 7]],
            ins=[cci], outs=[cco]), waits=[a_d1], ccname=ccname)
        return a_cc, cco

    def exchange_finish(ph, is_b, a_cc, cco, kvBD_free):
        a_d2 = P.add("sync", lambda eng: eng.dma_start(
            out=kvS[:, :], in_=cco[:, 0:1024]), waits=[a_cc], dma=True,
            dsem="dG", tag=f"{ph}.d2")
        a_d2b = None
        if is_b:
            a_d2b = P.add("sync", lambda eng: eng.dma_start(
                out=ksS[:, :], in_=cco[:, 1024:1032]), waits=[a_cc], dma=True,
                dsem="dG", tag=f"{ph}.d2b")
        a_z = P.add("gpsimd", lambda eng: eng.memset(kvBD[:, :], 0.0),
                    waits=kvBD_free, tag=f"{ph}.z")
        a_q = None
        for p in range(8):
            def em_q(eng, _p=p):
                eng.tensor_copy(out=kvBD[0:64, _p * 128:_p * 128 + 64],
                                in_=kvS[0:64, _p * 128:_p * 128 + 64])
                return eng.tensor_copy(out=kvBD[64:128, _p * 128 + 64:(_p + 1) * 128],
                                       in_=kvS[64:128, _p * 128 + 64:(_p + 1) * 128])
            a_q = P.add("vector", em_q, waits=[a_d2, a_d2b, a_z], tag=f"{ph}.q{p}")
        if is_b:
            a_zz = P.add("gpsimd", lambda eng: eng.memset(t1[:, :], 0.0),
                         waits=r_t1.fr(0), tag=f"{ph}.zz")
            for c in range(8):
                def em_ks(eng, _c=c):
                    eng.tensor_copy(out=t1[0:64, _c * 128 + 2 * _c:_c * 128 + 2 * _c + 1],
                                    in_=ksS[0:64, _c:_c + 1])
                    return eng.tensor_copy(
                        out=t1[64:128, _c * 128 + 2 * _c + 1:_c * 128 + 2 * _c + 2],
                        in_=ksS[64:128, _c:_c + 1])
                a_q = P.add("vector", em_ks, waits=[a_d2b, a_zz], tag=f"{ph}.ks{c}")
        return a_q

    def phase_p2(ph, is_b, qslot, oslot, kvbd_ready, side_work, fe_pre=None):
        qname = "wq_b" if is_b else "wq_a"
        oname = "wo_b" if is_b else "wo_a"
        boB = bB["bo_b" if is_b else "bo_a"]
        boB_rdy = bB[("bo_b" if is_b else "bo_a") + "_ready"]
        qbanks = [pb["A"], pb["B"]]
        gbanks = [pb["C"], pb["D"]]
        abanks = [pb["E"], pb["F"]]
        dbanks = [pb["C"], pb["D"]]
        rbanks = [pb["A"], pb["B"]]
        obanks = [pb["E"], pb["F"]]
        qcol0 = 48 if is_b else 32
        prev_omm = None
        fe_arts = (fe_pre if fe_pre is not None
                   else [frontend(ph, u, u, alpha=False) for u in range(4)])
        feA_next = [None, None]
        for s in range(NS):
            art_qf = [None] * 8
            art_ga = [None] * 8
            qg_arts = []
            if is_b:
                a_id = P.add("sync", lambda eng, _s=s: eng.dma_start(
                    out=idecS[:, :], in_=idec_in[:, _s * 512:(_s + 1) * 512]),
                    waits=r_idec.fr(0), dma=True, dsem="dI", tag=f"{ph}.id{s}")
            for c in range(8):
                qbank = qbanks[c % 2]
                a_qmm = P.add("tensor", mm_group(
                    qbank.ap[:, :],
                    lambda k, _c=c, _q=qslot: w_sl[_q][:, k, _c * 128:(_c + 1) * 128],
                    lambda k: xlnt[:, k, :]),
                    waits=fe_arts + wready[qname] + qbank.free,
                    tag=f"{ph}.qmm{s}{c}")
                qg_arts.append(a_qmm)
                if is_b:
                    a_qf = P.add("vector", lambda eng, _c=c, _b=qbank: eng.tensor_scalar(
                        out=qfT[:, _c, :], in0=_b.ap[:, :],
                        scalar1=auxfS[:, qcol0 + _c:qcol0 + _c + 1], scalar2=0.0,
                        op0=ALU.add, op1=ALU.max),
                        waits=[a_qmm, art_aux_f, prev_omm], tag=f"{ph}.qf{s}{c}")
                    qbank.setf(a_qf)
                else:
                    a_m = P.add("vector", lambda eng, _c=c, _b=qbank: eng.tensor_scalar(
                        out=ep[:, _c % 2, :], in0=_b.ap[:, :],
                        scalar1=auxfS[:, qcol0 + _c:qcol0 + _c + 1], scalar2=0.0,
                        op0=ALU.add, op1=ALU.min),
                        waits=[a_qmm, art_aux_f] + r_ep.fr(c), tag=f"{ph}.qm{s}{c}")
                    a_r = P.add("vector", lambda eng, _c=c, _b=qbank: eng.tensor_scalar(
                        out=rp[:, _c % 2, :], in0=_b.ap[:, :],
                        scalar1=auxfS[:, qcol0 + _c:qcol0 + _c + 1], scalar2=0.0,
                        op0=ALU.add, op1=ALU.max),
                        waits=[a_qmm, art_aux_f] + r_rp.fr(c),
                        tag=f"{ph}.qr{s}{c}")
                    qbank.setf([a_r])
                    a_dmp = None
                    if debug == 2 and s == 0 and c == 0 and not is_b:
                        a_dmp = P.add("sync", lambda eng: eng.dma_start(
                            out=dbg_out["dmn"], in_=ep[:, 0, :]),
                            waits=[a_m], dma=True, dsem="dDmp", tag=f"{ph}.dmn")
                        a_dmp = P.add("sync", lambda eng: eng.dma_start(
                            out=dbg_out["drp"], in_=rp[:, 0, :]),
                            waits=[a_r, a_dmp], dma=True, dsem="dDmp", tag=f"{ph}.drp")
                    a_e = P.add("scalar", lambda eng, _c=c: eng.activation(
                        out=ep[:, _c % 2, :], in_=ep[:, _c % 2, :], func=AF.Exp),
                        waits=[a_m, a_dmp], tag=f"{ph}.qe{s}{c}")
                    if debug == 2 and s == 0 and c == 0 and not is_b:
                        a_dmp = P.add("sync", lambda eng: eng.dma_start(
                            out=dbg_out["dep"], in_=ep[:, 0, :]),
                            waits=[a_e], dma=True, dsem="dDmp", tag=f"{ph}.dep")
                    a_qf = P.add("gpsimd", lambda eng, _c=c: eng.tensor_tensor(
                        out=qfT[:, _c, :], in0=rp[:, _c % 2, :], in1=ep[:, _c % 2, :],
                        op=ALU.add), waits=[a_e, a_r, prev_omm, a_dmp],
                        tag=f"{ph}.qf{s}{c}")
                    if debug == 2 and s == 0 and c == 0 and not is_b:
                        a_qf = P.add("sync", lambda eng: eng.dma_start(
                            out=dbg_out["dq0"], in_=qfT[:, 0, :]),
                            waits=[a_qf], dma=True, dsem="dDq", tag=f"{ph}.dq0")
                    r_rp.setf(c, a_qf)
                    r_ep.setf(c, a_qf)
                art_qf[c] = a_qf
                if debug == 2 and s == 0 and not is_b and c in (1, 3, 5, 7):
                    kk = {1: "dqA", 3: "dqB", 5: "dqC", 7: "dqD"}[c]
                    art_qf[c] = P.add("sync", lambda eng, _k=kk: eng.dma_start(
                        out=dbg_out[_k], in_=qfT[:, 0, :]),
                        waits=[a_qf], dma=True, dsem="dDq", tag=f"{ph}.{kk}")
            if not is_b:
                for c in range(8):
                    gbank = gbanks[c % 2]
                    a_gmm = P.add("tensor", mm_group(
                        gbank.ap[:, :],
                        lambda k, _c=c: w_sl[3][:, k, _c * 128:(_c + 1) * 128],
                        lambda k: xlnt[:, k, :]),
                        waits=wready["wg_a"] + gbank.free, tag=f"{ph}.gmm{s}{c}")
                    qg_arts.append(a_gmm)
                    a_g = P.add("scalar", lambda eng, _c=c, _b=gbank: eng.activation(
                        out=gateC[:, _c, :], in_=_b.ap[:, :], func=AF.Sigmoid,
                        bias=auxfS[:, 40 + _c:40 + _c + 1]),
                        waits=[a_gmm, art_aux_f] + r_gate.fr(c), tag=f"{ph}.g{s}{c}")
                    gbank.setf(a_g)
                    art_ga[c] = a_g
            a_dump_qf = None
            if is_b:
                dbank = dbanks[s % 2]
                a_dmm = P.add("tensor", mm_group(
                    dbank.ap[:, :],
                    lambda k: t1[:, k * 128:(k + 1) * 128],
                    lambda k: qfT[:, k, :]),
                    waits=[art_qf[7], kvbd_ready] + dbank.free, tag=f"{ph}.dmm{s}")
                a_ds = P.add("vector", lambda eng, _b=dbank: eng.tensor_tensor(
                    out=dens[0:16, :], in0=_b.ap[0:16, :], in1=idecS[:, :], op=ALU.add),
                    waits=[a_dmm, a_id], tag=f"{ph}.ds{s}")
                dbank.setf(a_ds)
                r_idec.setf(0, a_ds)
                a_rec = P.add("vector", lambda eng: eng.reciprocal(
                    out=dens[0:16, :], in_=dens[0:16, :]), waits=[a_ds],
                    tag=f"{ph}.rec{s}")
                a_rec = P.add("vector", lambda eng: eng.tensor_copy(
                    out=rec16[0:16, :], in_=dens[0:16, :]), waits=[a_rec],
                    tag=f"{ph}.rec16{s}")
            for c in range(8):
                if is_b:
                    rbank = rbanks[c % 2]
                    a_rbm = P.add("tensor", lambda eng, _c=c, _b=rbank: eng.matmul(
                        _b.ap[:, :], auxbS[:, 256 + _c * 128:256 + (_c + 1) * 128],
                        rec16[:, :], start=True, stop=True),
                        waits=[a_rec, art_aux_b, art_r16z] + rbank.free,
                        tag=f"{ph}.rbm{s}{c}")
                    a_rbe = P.add("vector", lambda eng, _c=c, _b=rbank: eng.tensor_copy(
                        out=recB[:, _c % 2, :], in_=_b.ap[:, :]),
                        waits=[a_rbm] + r_recB.fr(c), tag=f"{ph}.rbe{s}{c}")
                    rbank.setf(a_rbe)
                abank = abanks[c % 2]
                a_amm = P.add("tensor", lambda eng, _c=c, _b=abank: eng.matmul(
                    _b.ap[:, :], kvBD[:, _c * 128:(_c + 1) * 128], qfT[:, _c, :],
                    start=True, stop=True),
                    waits=[art_qf[c], kvbd_ready] + abank.free,
                    tag=f"{ph}.amm{s}{c}")
                if is_b:
                    a_ga = P.add("vector", lambda eng, _c=c, _b=abank: eng.tensor_tensor(
                        out=qfT[:, _c, :], in0=_b.ap[:, :], in1=recB[:, _c % 2, :],
                        op=ALU.mult), waits=[a_amm, a_rbe], tag=f"{ph}.ga{s}{c}")
                    r_recB.setf(c, a_ga)
                else:
                    a_ga = P.add("vector", lambda eng, _c=c, _b=abank: eng.tensor_tensor(
                        out=gateC[:, _c, :], in0=gateC[:, _c, :], in1=_b.ap[:, :],
                        op=ALU.mult), waits=[a_amm, art_ga[c]], tag=f"{ph}.ga{s}{c}")
                abank.setf(a_ga)
                art_ga[c] = a_ga
            if s + 1 < NS:
                feA_next = [frontend_a(ph, 4 * (s + 1) + u, alpha=False)
                            for u in range(2)]
            a_dump_ga = None
            for u in range(4):
                i = 4 * s + u
                for h in range(2):
                    obank = obanks[(2 * u + h) % 2]
                    ga_src = qfT if is_b else gateC
                    a_omm = P.add("tensor", mm_group(
                        obank.ap[:, :],
                        lambda k, _u=u, _g=ga_src: _g[:, k, _u * 128:(_u + 1) * 128],
                        lambda k, _h=h, _o=oslot: w_sl[_o][:, k, _h * 512:(_h + 1) * 512],
                        biasrow=boB[:, h * 512:(h + 1) * 512]),
                        waits=[art_ga[7], a_dump_ga, boB_rdy, art_aux_b]
                              + wready[oname] + obank.free,
                        tag=f"{ph}.omm{s}{u}{h}")
                    prev_omm = a_omm
                    a_x = P.add("vector", lambda eng, _i=i, _h=h, _b=obank:
                        eng.tensor_tensor(
                            out=X[:, _i, _h * 512:(_h + 1) * 512],
                            in0=X[:, _i, _h * 512:(_h + 1) * 512], in1=_b.ap[:, :],
                            op=ALU.add),
                        waits=[a_omm], tag=f"{ph}.ox{s}{u}{h}")
                    obank.setf(a_x)
                    Xready[i] = a_x
            for u in range(4):
                r_xlnt.setf(u, qg_arts[-1])
            if not is_b:
                for c2 in range(8):
                    r_gate.setf(c2, prev_omm)
            if s + 1 < NS:
                b0 = frontend_b(ph, 4 * (s + 1), 0, feA_next[0])
                b1 = frontend_b(ph, 4 * (s + 1) + 1, 1, feA_next[1])
                a2 = frontend_a(ph, 4 * (s + 1) + 2, alpha=False)
                a3 = frontend_a(ph, 4 * (s + 1) + 3, alpha=False)
                b2 = frontend_b(ph, 4 * (s + 1) + 2, 2, a2)
                b3 = frontend_b(ph, 4 * (s + 1) + 3, 3, a3)
                fe_arts = [b0, b1, b2, b3]
            if side_work:
                side_work.pop(0)(s)
        while side_work:
            side_work.pop(0)(NS - 1)
        return

    def phase_ffn(ph, final, side_work, fe_pre=None):
        banks = [pb["A"], pb["B"], pb["C"]]
        pi = 0
        fe_art = [None] * (NT + 2)
        fe_a = [None] * (NT + 2)
        if fe_pre is not None:
            fe_art[0], fe_art[1] = fe_pre
        else:
            fe_art[0] = frontend(ph, 0, 0, alpha=True)
            if NT > 1:
                fe_art[1] = frontend(ph, 1, 1, alpha=True)
        for i in range(NT):
            if i + 2 < NT:
                fe_a[i + 2] = frontend_a(ph, i + 2, alpha=True)
            a_xlt = fe_art[i]
            last_arts = []
            a_o_last = None
            mm_arts = []
            evac_work = []
            for h in range(2):
                bank = banks[pi % 3]
                pi += 1
                a_mm = P.add("tensor", mm_group(
                    bank.ap[:, :],
                    lambda k, _i=i: xlnt[:, k, (_i % 4) * 128:((_i % 4) + 1) * 128],
                    lambda k, _h=h: w_sl[1][:, k, _h * 512:(_h + 1) * 512],
                    biasrow=bB["fb"][:, h * 512:(h + 1) * 512]),
                    waits=[a_xlt, bB["fb_ready"], art_aux_b] + wready["ternT"]
                          + bank.free,
                    tag=f"{ph}.fmm{i}{h}")
                last_arts.append(a_mm)
                mm_arts.append((h, bank, a_mm))
            r_xlnt.setf(i % 4, last_arts[-1])
            if i + 2 < NT:
                fe_art[i + 2] = frontend_b(ph, i + 2, (i + 2) % 4, fe_a[i + 2])
            for h, bank, a_mm in mm_arts:
                if final:
                    a_o = P.add("vector", lambda eng, _i=i, _h=h, _b=bank:
                        eng.tensor_tensor(
                            out=xland[:, _h * 512:(_h + 1) * 512],
                            in0=X[:, _i, _h * 512:(_h + 1) * 512], in1=_b.ap[:, :],
                            op=ALU.add),
                        waits=[a_mm] + r_xland.fr(i), tag=f"{ph}.fo{i}{h}")
                    bank.setf(a_o)
                    a_o_last = a_o
                else:
                    a_x = P.add("vector", lambda eng, _i=i, _h=h, _b=bank:
                        eng.tensor_tensor(
                            out=X[:, _i, _h * 512:(_h + 1) * 512],
                            in0=X[:, _i, _h * 512:(_h + 1) * 512], in1=_b.ap[:, :],
                            op=ALU.add),
                        waits=[a_mm], tag=f"{ph}.fx{i}{h}")
                    bank.setf(a_x)
                    Xready[i] = a_x
            if final:
                a_st = P.add("sync", lambda eng, _i=i: eng.dma_start(
                    out=y_out[_i * 128:(_i + 1) * 128, :], in_=xland[:, :]),
                    waits=[a_o_last], dma=True, dsem="dO", tag=f"{ph}.st{i}")
                r_xland.setf(i, a_st)
            if side_work:
                side_work.pop(0)(i)
        while side_work:
            side_work.pop(0)(NT - 1)
        return

    side_tasks = []
    for i in range(12, NT, 2):
        side_tasks.append(lambda i2, _i=i: load_x_pair(_i))
    for c in range(NCH):
        side_tasks.append(lambda i, _c=c: wload_plain("ph1", "wq_a", 2, _c))
    for c in range(NCH):
        side_tasks.append(lambda i, _c=c: wload_plain("ph1", "wg_a", 3, _c))

    def wmean_chunk(c):
        r = c % 2
        a_d = P.add("sync", lambda eng, _c=c, _r=r: eng.dma_start(
            out=wland[:, _r, :], in_=w_hbm["wffn"][_c * 128:(_c + 1) * 128, :]),
            waits=r_wland.fr(c), dma=True, dsem=f"dW{c % 2}", tag=f"ph1.md{c}")
        a_a = P.add("vector", lambda eng, _c=c, _r=r: eng.tensor_reduce(
            out=absacc[:, _c:_c + 1], in_=wland[:, _r, :], axis=mybir.AxisListType.X,
            op=ALU.add, apply_absolute_value=True),
            waits=[a_d], tag=f"ph1.ma{c}")
        r_wland.setf(c, a_a)
        wready.setdefault("wmean", [None] * 8)[c] = a_a
    for c in range(NCH):
        side_tasks.append(lambda i, _c=c: wmean_chunk(_c))
    side1 = []
    ntasks = len(side_tasks)
    pos = 0
    for t in range(NT):
        take = (ntasks * (t + 1)) // NT - (ntasks * t) // NT
        chunk = side_tasks[pos:pos + take]
        pos += take
        side1.append(lambda i, _ch=tuple(chunk): [f(i) for f in _ch])

    kvb_a = [pb["D"], pb["E"]]
    art_kvS_a, _ = phase_p1("ph1", False, 0, 1, kvb_a, side1,
                            proj=[pb["A"], pb["B"], pb["C"], pb["F"]])
    wfree[0] = []
    wfree[0] = [art_kvS_a]
    wfree[1] = [art_kvS_a]

    a_dm1 = P.add("tensor", lambda eng: eng.matmul(
        pb["F"].ap[0:1, 0:8], onesf_col, absacc[:, :], start=True, stop=True),
        waits=wready["wmean"] + [art_aux_f] + pb["F"].free, tag="ph1.dm1")
    a_dr = P.add("vector", lambda eng: eng.tensor_reduce(
        out=dsc[0:1, 0:1], in_=pb["F"].ap[0:1, 0:8], axis=mybir.AxisListType.X,
        op=ALU.add), waits=[a_dm1], tag="ph1.dr")
    a_dsc = P.add("vector", lambda eng: eng.tensor_scalar(
        out=dsc[0:1, 1:2], in0=dsc[0:1, 0:1], scalar1=THRESH / float(D * D),
        scalar2=None, op0=ALU.mult), waits=[a_dr], tag="ph1.dsc")
    a_dbm = P.add("tensor", lambda eng: eng.matmul(
        pb["F"].ap[:, 8:9], onesf_row, dsc[0:1, 1:2], start=True, stop=True),
        waits=[a_dsc], tag="ph1.dbm")
    a_db = P.add("vector", lambda eng: eng.tensor_copy(
        out=deltaB[:, :], in_=pb["F"].ap[:, 8:9]), waits=[a_dbm], tag="ph1.db")
    a_ndb = P.add("vector", lambda eng: eng.tensor_scalar(
        out=ndeltaB[:, :], in0=deltaB[:, :], scalar1=-1.0, scalar2=None,
        op0=ALU.mult), waits=[a_db], tag="ph1.ndb")
    pb["F"].setf(a_db)
    art_deltaB[0] = a_db
    art_deltaB[1] = a_ndb

    P.barrier()
    xa_cc, xa_cco = exchange_start("phxa", False, art_kvS_a, "cc1", cca_i, cca_o)
    for c in range(NCH):
        wload_plain("ph2", "wo_a", 0, c)
    fe_pre2 = [frontend("ph2", u, u, alpha=False) for u in range(4)]
    kvbd_a = exchange_finish("phxa", False, xa_cc, xa_cco, [])
    if debug:
        P.add("sync", lambda eng: eng.dma_start(out=dbg_out["dkv"], in_=kvS[:, :]),
              waits=[kvbd_a], dma=True, dsem="dDkv", tag="dbg.dkv")
        kvbd_a = P.add("sync", lambda eng: eng.dma_start(
            out=dbg_out["dkvbd"], in_=kvBD[:, :]), waits=[kvbd_a], dma=True,
            dsem="dDkv", tag="dbg.dkvbd")

    side2 = [lambda s, _c=c: tern_chunk("ph2", _c) for c in range(NCH)]
    while len(side2) < NS:
        side2.append(lambda s: None)
    P.barrier()
    phase_p2("ph2", False, 2, 0, kvbd_a, side2, fe_pre=fe_pre2)
    wfree[2] = [Xready[NT - 1]]
    wfree[3] = [Xready[NT - 1]]
    wfree[0] = [Xready[NT - 1]]

    def dump_X(key):
        if not debug:
            return
        for i in range(NT):
            P.add("gpsimd", lambda eng, _i=i, _k=key: eng.dma_start(
                out=dbg_out[_k][_i * 128:(_i + 1) * 128, :], in_=X[:, _i, :]),
                waits=[Xready[i]], dma=True, dsem="dDX", tag=f"dbg{key}{i}")

    dump_X("d1")

    side3 = []
    for c in range(NCH):
        side3.append(lambda i, _c=c: wload_transposed("ph3", "wk_b", 2, _c))
    for c in range(NCH):
        side3.append(lambda i, _c=c: wload_transposed("ph3", "wv_b", 3, _c))
    while len(side3) < NT:
        side3.append(lambda i: None)
    fe_pre3 = [frontend("ph3", 0, 0, alpha=True),
               frontend("ph3", 1, 1, alpha=True)]
    P.barrier()
    phase_ffn("ph3", False, side3, fe_pre=fe_pre3)

    dump_X("d2")

    side4 = [lambda i, _c=c: wload_transposed("ph4", "wq_b", 0, _c)
             for c in range(NCH)]
    while len(side4) < NT:
        side4.append(lambda i: None)
    P.barrier()
    kvb_b = [pb["D"], pb["E"], pb["F"]]
    art_kvS_b, _ = phase_p1("ph4", True, 2, 3, kvb_b, side4)
    wfree[2] = [art_kvS_b]
    wfree[3] = [art_kvS_b]

    P.barrier()
    xb_cc, xb_cco = exchange_start("phxb", True, art_kvS_b, "cc2", ccb_i, ccb_o)
    for c in range(NCH):
        wload_transposed("ph5", "wo_b", 3, c)
    fe_pre5 = [frontend("ph5", u, u, alpha=False) for u in range(4)]
    kvbd_b = exchange_finish("phxb", True, xb_cc, xb_cco, [Xready[NT - 1]])

    P.barrier()
    phase_p2("ph5", True, 0, 3, kvbd_b, [], fe_pre=fe_pre5)

    dump_X("d3")

    fe_pre6 = [frontend("ph6", 0, 0, alpha=True),
               frontend("ph6", 1, 1, alpha=True)]
    P.barrier()
    phase_ffn("ph6", True, [], fe_pre=fe_pre6)

    fin_waits = [("dO", P.cnt.get("dO", 0))]
    if debug:
        fin_waits.append(("dDX", P.cnt.get("dDX", 0)))
    P.add("sync", lambda eng: eng.nop(),
          waits=[w for w in fin_waits if w[1] > 0], tag="fin")

    P.emit()
    es.close()
    return nc


def _prep_in_maps(inp, NT, nseq):
    T = NT * 128
    n_cores = 8
    x = np.ascontiguousarray(
        np.asarray(inp["L0_attn_a_x"], dtype=np.float32).reshape(-1, D))
    assert x.shape[0] == n_cores * T

    wmap = {
        "wq_a": inp["L0_attn_a_Wq"], "wk_a": inp["L0_attn_a_Wk"],
        "wv_a": inp["L0_attn_a_Wv"], "wg_a": inp["L0_attn_a_Wg"],
        "wo_a": inp["L0_attn_a_Wo"],
        "wq_b": inp["L0_attn_b_W_q"], "wk_b": inp["L0_attn_b_W_k"],
        "wv_b": inp["L0_attn_b_W_v"], "wo_b": inp["L0_attn_b_W_o"],
        "wffn": inp["L0_ffn_weight"],
    }
    wmap = {k: np.ascontiguousarray(np.asarray(v, dtype=np.float32))
            for k, v in wmap.items()}
    biasrows = np.ascontiguousarray(np.stack([
        np.asarray(inp["L0_attn_a_bk"], np.float32),
        np.asarray(inp["L0_attn_a_bv"], np.float32),
        np.asarray(inp["L0_attn_a_bo"], np.float32),
        np.asarray(inp["L0_attn_b_b_k"], np.float32),
        np.asarray(inp["L0_attn_b_b_v"], np.float32),
        np.asarray(inp["L0_attn_b_b_o"], np.float32),
        np.asarray(inp["L0_ffn_bias"], np.float32)]))
    bq_a = np.asarray(inp["L0_attn_a_bq"], np.float32)
    bg_a = np.asarray(inp["L0_attn_a_bg"], np.float32)
    bq_b = np.asarray(inp["L0_attn_b_b_q"], np.float32)
    alpha = np.asarray(inp["L0_ffn_alpha"], np.float32).reshape(-1)[0]

    sel = np.zeros((16, 1024), dtype=np.float32)
    for c in range(8):
        sel[2 * c, c * 128:c * 128 + 64] = 1.0
        sel[2 * c + 1, c * 128 + 64:(c + 1) * 128] = 1.0
    auxb = np.zeros((128, 1408), dtype=np.float32)
    auxb[:, 0:128] = np.eye(128, dtype=np.float32)
    auxb[:, 128:256] = 1.0
    auxb[0:16, 256:1280] = sel
    auxb[0, 1280:1408] = 1.0
    auxb = auxb.astype(ml_dtypes.bfloat16)

    in_maps = []
    for cid in range(n_cores):
        s0 = (cid % 2) * T
        n = s0 + np.arange(T, dtype=np.float64)
        dec = np.exp(-n / float(nseq)).astype(np.float32)
        idec = (EPS_B * np.exp(n / float(nseq))).astype(np.float32)
        auxf = np.zeros((128, 192), dtype=np.float32)
        auxf[:, 0:NT] = np.ascontiguousarray(dec.reshape(NT, 128).T)
        auxf[:, 32:40] = bq_a.reshape(8, 128).T
        auxf[:, 40:48] = bg_a.reshape(8, 128).T
        auxf[:, 48:56] = bq_b.reshape(8, 128).T
        auxf[:, 56] = alpha
        auxf[:, 57] = 1.0
        auxf[:, 58:186] = 1.0
        m = {"x": np.ascontiguousarray(x[cid * T:(cid + 1) * T]),
             "biasrows": biasrows,
             "auxf": np.ascontiguousarray(auxf),
             "auxb": np.ascontiguousarray(auxb),
             "idec16": np.ascontiguousarray(np.tile(idec, (16, 1)))}
        m.update(wmap)
        in_maps.append(m)
    return in_maps


def kernel(**inputs):
    NT = 32
    nseq = 8192
    in_maps = _prep_in_maps(inputs, NT, nseq)
    nc = build_nc(NT, nseq)
    res = run_bass_kernel_spmd(nc, in_maps, core_ids=list(range(8)))
    y = np.concatenate([np.asarray(res.results[c]["y"]) for c in range(8)], axis=0)
    return y.reshape(4, 8192, D).astype(np.float32)


# version 33
# speedup vs baseline: 1.0067x; 1.0028x over previous
import numpy as np
import ml_dtypes
import concourse.bass as bass
import concourse.mybir as mybir
from concourse.bass_utils import run_bass_kernel_spmd
from contextlib import ExitStack

F32 = mybir.dt.float32
BF16 = mybir.dt.bfloat16
AF = mybir.ActivationFunctionType
ALU = mybir.AluOpType

ENGINES = ("sync", "scalar", "vector", "tensor", "gpsimd")
HEADS = 16
D = 1024
NCH = 8
EPS_LN = 1e-5
EPS_B = 1e-6
THRESH = 0.7


class Prog:

    def __init__(self, nc, es):
        self.nc = nc
        self.es = es
        self.ops = {e: [] for e in ENGINES}
        self.sems = {}
        self.cnt = {}
        for name in ("sy", "sc", "ve", "te", "gp", "cc1", "cc2"):
            self.sems[name] = es.enter_context(nc.semaphore(name))
            self.cnt[name] = 0
        self.esem = {"sync": "sy", "scalar": "sc", "vector": "ve",
                     "tensor": "te", "gpsimd": "gp"}
        self.bars = []

    def add(self, engine, emit, waits=(), dma=False, dsem=None, tag=""):
        ws = []
        for w in waits:
            if w is None:
                continue
            s, c = w
            assert c <= self.cnt[s], f"wait-on-future {s}:{c}>{self.cnt[s]} @{tag}"
            if c > 0:
                ws.append((s, c))
        if dma:
            assert dsem is not None, f"dma op without dsem @{tag}"
            if dsem not in self.sems:
                self.sems[dsem] = self.es.enter_context(self.nc.semaphore(dsem))
                self.cnt[dsem] = 0
            sem, k = dsem, 16
        else:
            sem, k = self.esem[engine], 1
        self.cnt[sem] += k
        self.ops[engine].append((ws, emit, sem, k))
        return (sem, self.cnt[sem])

    def add_cc(self, emit, waits, ccname):
        ws = [w for w in waits if w is not None]
        for s, c in ws:
            assert c <= self.cnt[s]
        self.ops["gpsimd"].append((ws, emit, ccname, 1))
        self.cnt[ccname] = 1
        return (ccname, 1)

    def barrier(self):
        self.bars.append({e: len(self.ops[e]) for e in ENGINES})

    def emit(self):
        nc = self.nc
        cuts = self.bars + [{e: len(self.ops[e]) for e in ENGINES}]
        start = {e: 0 for e in ENGINES}
        seen = {e: {} for e in ENGINES}
        for cut in cuts:
            with nc.Block(no_gpsimd_drain=True) as block:
                for e in ENGINES:
                    seg = self.ops[e][start[e]:cut[e]]
                    start[e] = cut[e]
                    if not seg:
                        continue

                    def section(eng, _ops=tuple(seg), _seen=seen[e]):
                        for ws, emitfn, sem, k in _ops:
                            wm = {}
                            for (s, c) in ws:
                                wm[s] = max(wm.get(s, 0), c)
                            for s, c in wm.items():
                                if _seen.get(s, 0) >= c:
                                    continue
                                eng.wait_ge(self.sems[s], c)
                                _seen[s] = c
                            inst = emitfn(eng)
                            inst.then_inc(self.sems[sem], k)
                    getattr(block, e)(section)


class Rot:

    def __init__(self, n):
        self.n = n
        self.free = [[] for _ in range(n)]

    def fr(self, i):
        return list(self.free[i % self.n])

    def setf(self, i, arts):
        self.free[i % self.n] = [a for a in (arts if isinstance(arts, list) else [arts]) if a]


class Bank:

    def __init__(self, ap):
        self.ap = ap
        self.free = []

    def setf(self, arts):
        self.free = [a for a in (arts if isinstance(arts, list) else [arts]) if a]


def build_nc(NT, nseq, debug=False):
    T = NT * 128
    NS = NT // 4
    nc = bass.Bass("TRN2", target_bir_lowering=False,
                   dynamic_dma_scratch_size=4096)
    es = ExitStack()
    dbg_out = {}
    if debug:
        for k in ("d1", "d2", "d3"):
            dbg_out[k] = nc.dram_tensor(k, [NT * 128, D], BF16,
                                        kind="ExternalOutput").ap()
        dbg_out["dkv"] = nc.dram_tensor("dkv", [128, 1024], F32,
                                        kind="ExternalOutput").ap()
        dbg_out["dqf"] = nc.dram_tensor("dqf", [128, NCH, 512], BF16,
                                        kind="ExternalOutput").ap()
        dbg_out["dkb"] = nc.dram_tensor("dkb", [128, 512], F32,
                                        kind="ExternalOutput").ap()
        dbg_out["drp"] = nc.dram_tensor("drp", [128, 512], BF16,
                                        kind="ExternalOutput").ap()
        dbg_out["dep"] = nc.dram_tensor("dep", [128, 512], BF16,
                                        kind="ExternalOutput").ap()
        dbg_out["dmn"] = nc.dram_tensor("dmn", [128, 512], BF16,
                                        kind="ExternalOutput").ap()
        dbg_out["dq0"] = nc.dram_tensor("dq0", [128, 512], BF16,
                                        kind="ExternalOutput").ap()
        for kk in ("dqA", "dqB", "dqC", "dqD", "dga0", "dgt0"):
            dbg_out[kk] = nc.dram_tensor(kk, [128, 512], BF16,
                                         kind="ExternalOutput").ap()
        dbg_out["dkvbd"] = nc.dram_tensor("dkvbd", [128, 1024], BF16,
                                          kind="ExternalOutput").ap()
        dbg_out["dps"] = nc.dram_tensor("dps", [128, 512], F32,
                                        kind="ExternalOutput").ap()
        dbg_out["dga"] = nc.dram_tensor("dga", [128, NCH, 512], BF16,
                                        kind="ExternalOutput").ap()

    x_in = nc.dram_tensor("x", [T, D], F32, kind="ExternalInput").ap()
    w_hbm = {}
    for name in ("wq_a", "wk_a", "wv_a", "wg_a", "wo_a",
                 "wq_b", "wk_b", "wv_b", "wo_b", "wffn"):
        w_hbm[name] = nc.dram_tensor(name, [D, D], F32, kind="ExternalInput").ap()
    biasrows = nc.dram_tensor("biasrows", [7, D], F32, kind="ExternalInput").ap()
    auxf_in = nc.dram_tensor("auxf", [128, 192], F32, kind="ExternalInput").ap()
    auxb_in = nc.dram_tensor("auxb", [128, 1408], BF16, kind="ExternalInput").ap()
    idec_in = nc.dram_tensor("idec16", [16, T], F32, kind="ExternalInput").ap()
    y_out = nc.dram_tensor("y", [T, D], F32, kind="ExternalOutput").ap()
    cca_i = nc.dram_tensor("cca_i", [128, 1024], F32).ap()
    cca_o = nc.dram_tensor("cca_o", [128, 1024], F32).ap()
    ccb_i = nc.dram_tensor("ccb_i", [128, 1032], F32).ap()
    ccb_o = nc.dram_tensor("ccb_o", [128, 1032], F32).ap()

    import os as _os
    _dump_alloc = _os.environ.get("KDUMPALLOC")

    def sb(name, shape, dt):
        h = es.enter_context(nc.sbuf_tensor(name, shape, dt))
        if _dump_alloc:
            m = nc.lookup_mloc(h)
            import math
            per = math.prod(shape[1:]) * mybir.dt.size(dt)
            print(f"SB {name:8s} addr={m.addr:7d} size={per:6d} end={m.addr + per}")
        return h
    X = sb("X", [128, NT, D], BF16)
    w_sl = [sb(f"wsl{j}", [128, NCH, D], BF16) for j in range(4)]
    bB = {n: sb("bB_" + n, [128, D], BF16)
          for n in ("bk_a", "bv_a", "bo_a", "bk_b", "bv_b", "bo_b", "fb")}
    auxfS = sb("auxfS", [128, 192], F32)
    auxbS = sb("auxbS", [128, 1408], BF16)
    idecS = sb("idecS", [16, 512], F32)
    kvS = sb("kvS", [128, 1024], F32)
    ksS = sb("ksS", [128, 8], F32)
    kvBD = sb("kvBD", [128, 1024], BF16)
    dsc = sb("dsc", [1, 8], F32)
    deltaB = sb("deltaB", [128, 1], F32)
    ndeltaB = sb("ndeltaB", [128, 1], F32)
    epscol = sb("epscol", [128, 1], F32)
    absacc = sb("absacc", [128, 8], F32)
    rec16 = sb("rec16", [128, 512], BF16)
    dens = sb("dens", [16, 512], F32)
    xland = sb("xland", [128, D], F32)
    wland = sb("wland", [128, 2, D], F32)
    xln = sb("xln", [128, 2, D], BF16)
    xlnt = sb("xlnt", [128, NCH, 512], BF16)
    bn6 = sb("bn6", [128, 2, 12], F32)
    mv = sb("mv", [128, 2, 2], F32)
    nmean = sb("nmean", [128, 2], F32)
    rstd = sb("rstd", [128, 2], F32)
    rstd2 = sb("rstd2", [128, 2], F32)
    ep = sb("ep", [128, 2, 512], BF16)
    rp = sb("rp", [128, 2, 512], BF16)
    kf = sb("kf", [128, 2, D], BF16)
    vf = sb("vf", [128, 2, 8, 129], BF16)
    qfT = sb("qfT", [128, NCH, 512], BF16)
    gateC = sb("gateC", [128, 8, 512], BF16)
    recB = sb("recB", [128, 2, 512], BF16)
    t1 = sb("t1", [128, D], BF16)
    t2 = sb("t2", [128, D], BF16)
    dpsS = sb("dpsS", [128, 512], F32) if debug == 2 else None

    pT = [es.enter_context(nc.psum_tensor(f"pT{j}", [128, NCH, 128], BF16))
          for j in range(2)]
    pb = {n: Bank(es.enter_context(nc.psum_tensor("p" + n, [128, 512], F32)))
          for n in ("A", "B", "C", "D", "E", "F")}
    pTb = [Bank(pT[0]), Bank(pT[1])]
    pTb[0].ap = pT[0]
    pTb[1].ap = pT[1]

    P = Prog(nc, es)

    ident = auxbS[:, 0:128]
    onesb = auxbS[0:1, 128:256]
    e0sel = auxbS[:, 1280:1408]
    onesf_col = auxfS[:, 57:58]
    onesf_row = auxfS[0:1, 58:186]
    alpha_col = auxfS[:, 56:57]

    r_xland = Rot(1)
    r_wland = Rot(2)
    r_xln = Rot(2)
    r_bn6 = Rot(2)
    r_mv = Rot(2)
    r_cols = Rot(2)
    r_pT = Rot(2)
    r_xlnt = Rot(4)
    r_kb = Rot(1)
    r_ep = Rot(2)
    r_rp = Rot(2)
    r_tmp = Rot(2)
    r_kf = Rot(2)
    r_vf = Rot(2)
    r_gate = Rot(8)
    r_recB = Rot(2)
    r_t1 = Rot(1)
    r_t2 = Rot(1)
    r_idec = Rot(1)
    r_browb = Rot(1)

    Xready = [None] * NT
    art_aux = None
    art_eps = None

    wready = {}
    wfree = {j: [] for j in range(4)}

    def mm_group(bankap, lhs_fn, rhs_fn, n=NCH, biasrow=None):
        def emit(eng, _b=bankap, _l=lhs_fn, _r=rhs_fn, _n=n, _br=biasrow):
            inst = None
            if _br is not None:
                inst = eng.matmul(_b, e0sel, _br, start=True, stop=False)
            for k in range(_n):
                inst = eng.matmul(_b, _l(k), _r(k), start=(k == 0 and _br is None),
                                  stop=(k == _n - 1))
            return inst
        return emit

    def frontend_a(ph, i, alpha=False):
        s2 = i % 2
        a_st = P.add("vector", lambda eng, _i=i, _s=s2: (
            eng.bn_stats(out=bn6[:, _s, 0:6], in_=X[:, _i, 0:512]),
            eng.bn_stats(out=bn6[:, _s, 6:12], in_=X[:, _i, 512:1024]))[-1],
            waits=r_bn6.fr(i) + [Xready[i]], tag=f"{ph}.bnst{i}")
        a_ag = P.add("vector", lambda eng, _s=s2: eng.bn_aggr(
            out=mv[:, _s, :], in_=bn6[:, _s, :]),
            waits=[a_st] + r_mv.fr(i), tag=f"{ph}.bnag{i}")
        r_bn6.setf(i, a_ag)

        a_r1 = P.add("scalar", lambda eng, _s=s2: (
            eng.activation(out=nmean[:, _s:_s + 1], in_=mv[:, _s, 0:1],
                           func=AF.Copy, scale=-1.0),
            eng.activation(out=rstd[:, _s:_s + 1], in_=mv[:, _s, 1:2],
                           func=AF.Ln, bias=epscol[:, :]))[-1],
            waits=[a_ag, art_eps] + r_cols.fr(i), tag=f"{ph}.rstd1_{i}")
        a_rs = P.add("scalar", lambda eng, _s=s2: eng.activation(
            out=rstd[:, _s:_s + 1], in_=rstd[:, _s:_s + 1], func=AF.Exp,
            scale=-0.5), waits=[a_r1], tag=f"{ph}.rstd2_{i}")
        if alpha:
            a_rs = P.add("scalar", lambda eng, _s=s2: eng.activation(
                out=rstd2[:, _s:_s + 1], in_=rstd[:, _s:_s + 1],
                func=AF.Copy, scale=alpha_col), waits=[a_rs],
                tag=f"{ph}.rstd3_{i}")
        r_mv.setf(i, a_rs)
        rcol = rstd2 if alpha else rstd
        a_xln = P.add("gpsimd", lambda eng, _i=i, _s=s2, _r=rcol: eng.tensor_scalar(
            out=xln[:, _s, :], in0=X[:, _i, :], scalar1=nmean[:, _s:_s + 1],
            scalar2=_r[:, _s:_s + 1], op0=ALU.add, op1=ALU.mult),
            waits=[a_rs] + r_xln.fr(i), tag=f"{ph}.norm{i}")
        r_cols.setf(i, a_xln)
        return a_xln

    def frontend_b(ph, i, q, a_xln):
        s2 = i % 2
        tb = pTb[i % 2]
        def em_tr(eng, _s=s2, _tb=tb):
            inst = None
            for c in range(NCH):
                inst = eng.transpose(_tb.ap[:, c, :], xln[:, _s, c * 128:(c + 1) * 128],
                                     ident)
            return inst
        a_tr = P.add("tensor", em_tr, waits=[a_xln, art_aux] + tb.free,
                     tag=f"{ph}.tr{i}")
        r_xln.setf(i, a_tr)
        a_xlt = P.add("scalar", lambda eng, _tb=tb, _q=q: eng.copy(
            out=xlnt[:, :, _q * 128:(_q + 1) * 128], in_=_tb.ap[:, :, :]),
            waits=[a_tr] + r_xlnt.fr(q), tag=f"{ph}.xev{i}")
        tb.setf(a_xlt)
        return a_xlt

    def frontend(ph, i, q, alpha=False):
        return frontend_b(ph, i, q, frontend_a(ph, i, alpha))

    def wload_plain(ph, name, slot, c):
        if c % 2 == 1:
            return
        a_d = P.add("sync", lambda eng, _n=name, _c=c: eng.dma_start(
            out=wland[:, :, :],
            in_=w_hbm[_n][_c * 128:(_c + 2) * 128, :].rearrange(
                "(t p) d -> p t d", p=128)),
            waits=r_wland.fr(0) + r_wland.fr(1), dma=True, dsem="dW0",
            tag=f"{ph}.wd{name}{c}")
        a_c0 = P.add("scalar", lambda eng, _s=slot, _c=c: eng.copy(
            out=w_sl[_s][:, _c, :], in_=wland[:, 0, :]),
            waits=[a_d] + wfree[slot], tag=f"{ph}.wc{name}{c}")
        a_c1 = P.add("vector", lambda eng, _s=slot, _c=c: eng.tensor_copy(
            out=w_sl[_s][:, _c + 1, :], in_=wland[:, 1, :]),
            waits=[a_d] + wfree[slot], tag=f"{ph}.wc{name}{c + 1}")
        r_wland.setf(0, a_c0)
        r_wland.setf(1, a_c1)
        wready.setdefault(name, [None] * 8)[c] = a_c0
        wready[name][c + 1] = a_c1

    def wload_transposed(ph, name, slot, c):
        r = c % 2
        a_d = P.add("sync", lambda eng, _n=name, _c=c, _r=r: eng.dma_start(
            out=wland[:, _r, :], in_=w_hbm[_n][_c * 128:(_c + 1) * 128, :]),
            waits=r_wland.fr(c), dma=True, dsem=f"dW{c % 2}", tag=f"{ph}.twd{name}{c}")
        a_c = P.add("scalar", lambda eng, _r=r: eng.copy(out=t2[:, :], in_=wland[:, _r, :]),
                    waits=[a_d] + r_t2.fr(0), tag=f"{ph}.twc{name}{c}")
        r_wland.setf(c, a_c)
        tb = pTb[c % 2]
        def em_tr(eng, _tb=tb):
            inst = None
            for b in range(NCH):
                inst = eng.transpose(_tb.ap[:, b, :], t2[:, b * 128:(b + 1) * 128], ident)
            return inst
        a_tr = P.add("tensor", em_tr, waits=[a_c, art_aux] + tb.free,
                     tag=f"{ph}.ttr{name}{c}")
        r_t2.setf(0, a_tr)
        a_ev = P.add("scalar", lambda eng, _s=slot, _c=c, _tb=tb: eng.copy(
            out=w_sl[_s][:, :, _c * 128:(_c + 1) * 128], in_=_tb.ap[:, :, :]),
            waits=[a_tr] + wfree[slot], tag=f"{ph}.tte{name}{c}")
        tb.setf(a_ev)
        wready.setdefault(name, [None] * 8)[c] = a_ev

    def tern_chunk(ph, c):
        r = c % 2
        a_d = P.add("sync", lambda eng, _c=c, _r=r: eng.dma_start(
            out=wland[:, _r, :], in_=w_hbm["wffn"][_c * 128:(_c + 1) * 128, :]),
            waits=r_wland.fr(c), dma=True, dsem=f"dW{c % 2}", tag=f"{ph}.td{c}")
        a_1 = P.add("vector", lambda eng, _r=r: eng.tensor_scalar(
            out=t1[:, :], in0=wland[:, _r, :], scalar1=deltaB[:, :], scalar2=None,
            op0=ALU.is_gt), waits=[a_d, art_deltaB[0]] + r_t1.fr(0),
            tag=f"{ph}.t1{c}")
        a_2 = P.add("vector", lambda eng, _r=r: eng.tensor_scalar(
            out=t2[:, :], in0=wland[:, _r, :], scalar1=ndeltaB[:, :], scalar2=None,
            op0=ALU.is_lt), waits=[a_d, art_deltaB[1]] + r_t2.fr(0),
            tag=f"{ph}.t2{c}")
        r_wland.setf(c, a_2)
        a_tc = P.add("vector", lambda eng: eng.tensor_tensor(
            out=t1[:, :], in0=t1[:, :], in1=t2[:, :], op=ALU.subtract),
            waits=[a_2], tag=f"{ph}.tsub{c}")
        r_t2.setf(0, a_tc)
        tb = pTb[c % 2]
        def em_tr(eng, _tb=tb):
            inst = None
            for b in range(NCH):
                inst = eng.transpose(_tb.ap[:, b, :], t1[:, b * 128:(b + 1) * 128], ident)
            return inst
        a_tr = P.add("tensor", em_tr, waits=[a_tc, art_aux] + tb.free,
                     tag=f"{ph}.ttr{c}")
        r_t1.setf(0, a_tr)
        a_ev = P.add("scalar", lambda eng, _c=c, _tb=tb: eng.copy(
            out=w_sl[1][:, :, _c * 128:(_c + 1) * 128], in_=_tb.ap[:, :, :]),
            waits=[a_tr] + wfree[1], tag=f"{ph}.tev{c}")
        tb.setf(a_ev)
        wready.setdefault("ternT", [None] * 8)[c] = a_ev

    art_aux_f = P.add("sync", lambda eng: eng.dma_start(out=auxfS[:, :], in_=auxf_in),
                      dma=True, dsem="dAf", tag="ph0.auxf")
    art_aux_b = P.add("sync", lambda eng: eng.dma_start(out=auxbS[:, :], in_=auxb_in),
                      dma=True, dsem="dAb", tag="ph0.auxb")
    art_aux = art_aux_b
    art_eps = P.add("gpsimd", lambda eng: eng.memset(epscol[:, :], EPS_LN),
                    tag="ph0.eps")
    art_vf_ones = P.add("gpsimd", lambda eng: eng.memset(vf[:, :, :, 128:129], 1.0),
                        tag="ph0.vfones")
    art_r16z = P.add("gpsimd", lambda eng: eng.memset(rec16[:, :], 0.0),
                     tag="ph0.r16z")

    xring = {"j": 0}

    def load_x_pair(i):
        for t in (i, i + 1):
            s = xring["j"] % 3
            xring["j"] += 1
            if s < 2:
                land_ap = wland[:, s, :]
                frees, track, slot = r_wland.fr(s), r_wland, s
            else:
                land_ap = xland[:, :]
                frees, track, slot = r_xland.fr(0), r_xland, 0
            a_d = P.add("sync", lambda eng, _t=t, _ap=land_ap: eng.dma_start(
                out=_ap, in_=x_in[_t * 128:(_t + 1) * 128, :]),
                waits=frees, dma=True, dsem=f"dX{s}", tag=f"ph0.xd{t}")
            if t % 2 == 0:
                a_c = P.add("scalar", lambda eng, _t=t, _ap=land_ap: eng.copy(
                    out=X[:, _t, :], in_=_ap), waits=[a_d], tag=f"ph0.xc{t}")
            else:
                a_c = P.add("vector", lambda eng, _t=t, _ap=land_ap: eng.tensor_copy(
                    out=X[:, _t, :], in_=_ap), waits=[a_d], tag=f"ph0.xc{t}")
            track.setf(slot, a_c)
            Xready[t] = a_c

    def load_x_tile(i):
        load_x_pair(i)
    for i in range(0, 4, 2):
        load_x_pair(i)

    bnames = ("bk_a", "bv_a", "bo_a", "bk_b", "bv_b", "bo_b", "fb")
    for j, bn in enumerate(bnames):
        a_d = P.add("sync", lambda eng, _j=j: eng.dma_start(
            out=wland[0:1, _j % 2, :], in_=biasrows[_j:_j + 1, :]),
            waits=r_wland.fr(j), dma=True, dsem=f"dW{j % 2}", tag=f"ph0.bd{j}")
        a_c = P.add("vector", lambda eng, _j=j: eng.tensor_copy(
            out=xln[0:1, 0, :], in_=wland[0:1, _j % 2, :]),
            waits=[a_d] + r_browb.fr(0), tag=f"ph0.bc{j}")
        r_wland.setf(j, a_c)
        arts = []
        for h in range(2):
            bank = pb["A"] if h == 0 else pb["B"]
            a_m = P.add("tensor", lambda eng, _h=h, _b=bank: eng.matmul(
                _b.ap[:, :], onesb, xln[0:1, 0, _h * 512:(_h + 1) * 512],
                start=True, stop=True),
                waits=[a_c, art_aux_b] + bank.free, tag=f"ph0.bm{j}{h}")
            a_e = P.add("vector", lambda eng, _h=h, _b=bank, _bn=bn: eng.tensor_copy(
                out=bB[_bn][:, _h * 512:(_h + 1) * 512], in_=_b.ap[:, :]),
                waits=[a_m], tag=f"ph0.be{j}{h}")
            bank.setf(a_e)
            arts.append(a_e)
        r_browb.setf(0, arts)
        bB[bn + "_ready"] = arts[-1]
    r_xln.setf(0, list(r_browb.free[0]))

    for c in range(NCH):
        wload_plain("ph0", "wk_a", 0, c)
    for c in range(NCH):
        wload_plain("ph0", "wv_a", 1, c)
    for i in range(4, min(12, NT), 2):
        load_x_pair(i)

    art_deltaB = [None, None]

    def phase_p1(ph, is_b, kslot, vslot, kvbanks, side_work, proj=None,
                 fe_pre=None):
        kname = "wk_b" if is_b else "wk_a"
        vname = "wv_b" if is_b else "wv_a"
        bkB = bB["bk_b" if is_b else "bk_a"]
        bvB = bB["bv_b" if is_b else "bv_a"]
        bkB_rdy = bB[("bk_b" if is_b else "bk_a") + "_ready"]
        bvB_rdy = bB[("bv_b" if is_b else "bv_a") + "_ready"]
        proj = proj or [pb["A"], pb["B"], pb["C"]]
        pi = 0
        art_kv_last = None
        if is_b:
            pair_off = [(0, 0), (0, 129), (0, 258), (1, 0), (1, 129), (1, 258),
                        (2, 0), (2, 129)]
            first_in_bank = (0, 3, 6)
        else:
            pair_off = [(0, 0), (0, 128), (0, 256), (0, 384), (1, 0), (1, 128),
                        (1, 256), (1, 384)]
            first_in_bank = (0, 4)

        fe_art = [None] * (NT + 2)
        fe_a = [None] * (NT + 2)
        if fe_pre is not None:
            fe_art[0], fe_art[1] = fe_pre
        else:
            fe_art[0] = frontend(ph, 0, 0, alpha=False)
            if NT > 1:
                fe_art[1] = frontend(ph, 1, 1, alpha=False)
        pend_kv = None
        for i in range(NT):
            if i + 2 < NT:
                fe_a[i + 2] = frontend_a(ph, i + 2, alpha=False)
            a_xlt = fe_art[i]
            art_kf = [None, None]
            art_vf = [None, None]
            proj_arts = []
            for h in range(2):
                bank = proj[pi % len(proj)]
                pi += 1
                a_mm = P.add("tensor", mm_group(
                    bank.ap[:, :],
                    lambda k, _i=i: xlnt[:, k, (_i % 4) * 128:((_i % 4) + 1) * 128],
                    lambda k, _h=h, _s=kslot: w_sl[_s][:, k, _h * 512:(_h + 1) * 512],
                    biasrow=bkB[:, h * 512:(h + 1) * 512]),
                    waits=[a_xlt, bkB_rdy, art_aux_b] + wready[kname] + bank.free,
                    tag=f"{ph}.kmm{i}{h}")
                proj_arts.append(a_mm)
                if is_b:
                    a_kf = P.add("vector", lambda eng, _h=h, _i=i, _b=bank:
                        eng.tensor_scalar(
                            out=kf[:, _i % 2, _h * 512:(_h + 1) * 512],
                            in0=_b.ap[:, :], scalar1=auxfS[:, _i:_i + 1],
                            scalar2=0.0, op0=ALU.mult, op1=ALU.max),
                        waits=[a_mm, art_aux_f] + r_kf.fr(i), tag=f"{ph}.kf{i}{h}")
                    bank.setf(a_kf)
                else:
                    a_m = P.add("vector", lambda eng, _h=h, _b=bank: eng.tensor_scalar(
                        out=ep[:, _h, :], in0=_b.ap[:, :], scalar1=0.0, scalar2=None,
                        op0=ALU.min), waits=[a_mm] + r_ep.fr(h), tag=f"{ph}.km{i}{h}")
                    a_r = P.add("vector", lambda eng, _h=h, _b=bank: eng.tensor_scalar(
                        out=rp[:, _h, :], in0=_b.ap[:, :], scalar1=0.0, scalar2=None,
                        op0=ALU.max), waits=[a_mm] + r_rp.fr(h), tag=f"{ph}.kr{i}{h}")
                    bank.setf([a_r])
                    a_e = P.add("scalar", lambda eng, _h=h: eng.activation(
                        out=ep[:, _h, :], in_=ep[:, _h, :], func=AF.Exp),
                        waits=[a_m], tag=f"{ph}.ke{i}{h}")
                    a_kf = P.add("gpsimd", lambda eng, _h=h, _i=i: eng.tensor_tensor(
                        out=kf[:, _i % 2, _h * 512:(_h + 1) * 512], in0=rp[:, _h, :],
                        in1=ep[:, _h, :], op=ALU.add),
                        waits=[a_e, a_r] + r_kf.fr(i), tag=f"{ph}.kf{i}{h}")
                    r_rp.setf(h, a_kf)
                    r_ep.setf(h, a_kf)
                art_kf[h] = a_kf
            if pend_kv is not None:
                pend_kv()
                pend_kv = None
            if i + 2 < NT:
                fe_art[i + 2] = frontend_b(ph, i + 2, (i + 2) % 4, fe_a[i + 2])
            for h in range(2):
                bank = proj[pi % len(proj)]
                pi += 1
                a_mm = P.add("tensor", mm_group(
                    bank.ap[:, :],
                    lambda k, _i=i: xlnt[:, k, (_i % 4) * 128:((_i % 4) + 1) * 128],
                    lambda k, _h=h, _s=vslot: w_sl[_s][:, k, _h * 512:(_h + 1) * 512]),
                    waits=[a_xlt] + wready[vname] + bank.free,
                    tag=f"{ph}.vmm{i}{h}")
                proj_arts.append(a_mm)
                a_v = P.add("vector", lambda eng, _h=h, _i=i, _b=bank: eng.tensor_tensor(
                    out=vf[:, _i % 2, 4 * _h:4 * _h + 4, 0:128], in0=_b.ap[:, :],
                    in1=bvB[:, _h * 512:(_h + 1) * 512], op=ALU.add),
                    waits=[a_mm, bvB_rdy] + r_vf.fr(i), tag=f"{ph}.vf{i}{h}")
                bank.setf(a_v)
                art_vf[h] = a_v
            r_xlnt.setf(i % 4, proj_arts[-1])

            def mk_kv(_i, _kf, _vf):
                def emit_kv():
                    nonlocal art_kv_last
                    ncol = 129 if is_b else 128
                    def em_kv(eng, _i=_i, _po=pair_off, _nc=ncol,
                              _fib=first_in_bank):
                        inst = None
                        for p in range(8):
                            bidx, off = _po[p]
                            inst = eng.matmul(
                                kvbanks[bidx].ap[:, off:off + _nc],
                                kf[:, _i % 2, p * 128:(p + 1) * 128],
                                vf[:, _i % 2, p, 0:_nc],
                                start=(_i == 0 and p in _fib),
                                stop=(_i == NT - 1),
                                skip_group_check=True)
                        return inst
                    kvw = [_kf[0], _kf[1], _vf[0], _vf[1]]
                    if _i == 0:
                        for b in kvbanks:
                            kvw += b.free
                        kvw.append(art_vf_ones)
                    a_kv = P.add("tensor", em_kv, waits=kvw, tag=f"{ph}.kv{_i}")
                    r_kf.setf(_i, a_kv)
                    r_vf.setf(_i, a_kv)
                    art_kv_last = a_kv
                return emit_kv
            pend_kv = mk_kv(i, art_kf, art_vf)
            if side_work:
                side_work.pop(0)(i)
        pend_kv()
        while side_work:
            side_work.pop(0)(NT - 1)
        evs = []
        for p in range(8):
            bidx, off = pair_off[p]
            def em_ev(eng, _p=p, _b=kvbanks[bidx], _o=off, _isb=is_b):
                inst = eng.copy(out=kvS[:, _p * 128:(_p + 1) * 128],
                                in_=_b.ap[:, _o:_o + 128])
                if _isb:
                    inst = eng.copy(out=ksS[:, _p:_p + 1], in_=_b.ap[:, _o + 128:_o + 129])
                return inst
            a_e = P.add("scalar", em_ev, waits=[art_kv_last], tag=f"{ph}.kvev{p}")
            evs.append(a_e)
        for b in kvbanks:
            b.setf(evs[-1])
        return evs[-1], proj_arts

    def exchange_start(ph, is_b, art_kvS, ccname, cci, cco):
        a_d1 = P.add("sync", lambda eng: eng.dma_start(
            out=cci[:, 0:1024], in_=kvS[:, :]), waits=[art_kvS], dma=True,
            dsem="dG", tag=f"{ph}.d1")
        if is_b:
            a_d1 = P.add("sync", lambda eng: eng.dma_start(
                out=cci[:, 1024:1032], in_=ksS[:, :]), waits=[art_kvS], dma=True,
                dsem="dG", tag=f"{ph}.d1b")
        a_cc = P.add_cc(lambda eng: eng.collective_compute(
            "AllReduce", ALU.add,
            replica_groups=[[0, 1], [2, 3], [4, 5], [6, 7]],
            ins=[cci], outs=[cco]), waits=[a_d1], ccname=ccname)
        return a_cc, cco

    def exchange_finish(ph, is_b, a_cc, cco, kvBD_free):
        a_d2 = P.add("sync", lambda eng: eng.dma_start(
            out=kvS[:, :], in_=cco[:, 0:1024]), waits=[a_cc], dma=True,
            dsem="dG", tag=f"{ph}.d2")
        a_d2b = None
        if is_b:
            a_d2b = P.add("sync", lambda eng: eng.dma_start(
                out=ksS[:, :], in_=cco[:, 1024:1032]), waits=[a_cc], dma=True,
                dsem="dG", tag=f"{ph}.d2b")
        a_z = P.add("gpsimd", lambda eng: eng.memset(kvBD[:, :], 0.0),
                    waits=kvBD_free, tag=f"{ph}.z")
        a_q = None
        for p in range(8):
            def em_q(eng, _p=p):
                eng.tensor_copy(out=kvBD[0:64, _p * 128:_p * 128 + 64],
                                in_=kvS[0:64, _p * 128:_p * 128 + 64])
                return eng.tensor_copy(out=kvBD[64:128, _p * 128 + 64:(_p + 1) * 128],
                                       in_=kvS[64:128, _p * 128 + 64:(_p + 1) * 128])
            a_q = P.add("vector", em_q, waits=[a_d2, a_d2b, a_z], tag=f"{ph}.q{p}")
        if is_b:
            a_zz = P.add("gpsimd", lambda eng: eng.memset(t1[:, :], 0.0),
                         waits=r_t1.fr(0), tag=f"{ph}.zz")
            for c in range(8):
                def em_ks(eng, _c=c):
                    eng.tensor_copy(out=t1[0:64, _c * 128 + 2 * _c:_c * 128 + 2 * _c + 1],
                                    in_=ksS[0:64, _c:_c + 1])
                    return eng.tensor_copy(
                        out=t1[64:128, _c * 128 + 2 * _c + 1:_c * 128 + 2 * _c + 2],
                        in_=ksS[64:128, _c:_c + 1])
                a_q = P.add("vector", em_ks, waits=[a_d2b, a_zz], tag=f"{ph}.ks{c}")
        return a_q

    def phase_p2(ph, is_b, qslot, oslot, kvbd_ready, side_work, fe_pre=None):
        qname = "wq_b" if is_b else "wq_a"
        oname = "wo_b" if is_b else "wo_a"
        boB = bB["bo_b" if is_b else "bo_a"]
        boB_rdy = bB[("bo_b" if is_b else "bo_a") + "_ready"]
        qbanks = [pb["A"], pb["B"]]
        gbanks = [pb["C"], pb["D"]]
        abanks = [pb["E"], pb["F"]]
        dbanks = [pb["C"], pb["D"]]
        rbanks = [pb["A"], pb["B"]]
        obanks = [pb["E"], pb["F"]]
        qcol0 = 48 if is_b else 32
        prev_omm = None
        fe_arts = (fe_pre if fe_pre is not None
                   else [frontend(ph, u, u, alpha=False) for u in range(4)])
        feA_next = [None, None]
        for s in range(NS):
            art_qf = [None] * 8
            art_ga = [None] * 8
            qg_arts = []
            if is_b:
                a_id = P.add("sync", lambda eng, _s=s: eng.dma_start(
                    out=idecS[:, :], in_=idec_in[:, _s * 512:(_s + 1) * 512]),
                    waits=r_idec.fr(0), dma=True, dsem="dI", tag=f"{ph}.id{s}")
            for c in range(8):
                qbank = qbanks[c % 2]
                a_qmm = P.add("tensor", mm_group(
                    qbank.ap[:, :],
                    lambda k, _c=c, _q=qslot: w_sl[_q][:, k, _c * 128:(_c + 1) * 128],
                    lambda k: xlnt[:, k, :]),
                    waits=fe_arts + wready[qname] + qbank.free,
                    tag=f"{ph}.qmm{s}{c}")
                qg_arts.append(a_qmm)
                if is_b:
                    a_qf = P.add("vector", lambda eng, _c=c, _b=qbank: eng.tensor_scalar(
                        out=qfT[:, _c, :], in0=_b.ap[:, :],
                        scalar1=auxfS[:, qcol0 + _c:qcol0 + _c + 1], scalar2=0.0,
                        op0=ALU.add, op1=ALU.max),
                        waits=[a_qmm, art_aux_f, prev_omm], tag=f"{ph}.qf{s}{c}")
                    qbank.setf(a_qf)
                else:
                    a_m = P.add("vector", lambda eng, _c=c, _b=qbank: eng.tensor_scalar(
                        out=ep[:, _c % 2, :], in0=_b.ap[:, :],
                        scalar1=auxfS[:, qcol0 + _c:qcol0 + _c + 1], scalar2=0.0,
                        op0=ALU.add, op1=ALU.min),
                        waits=[a_qmm, art_aux_f] + r_ep.fr(c), tag=f"{ph}.qm{s}{c}")
                    a_r = P.add("vector", lambda eng, _c=c, _b=qbank: eng.tensor_scalar(
                        out=rp[:, _c % 2, :], in0=_b.ap[:, :],
                        scalar1=auxfS[:, qcol0 + _c:qcol0 + _c + 1], scalar2=0.0,
                        op0=ALU.add, op1=ALU.max),
                        waits=[a_qmm, art_aux_f] + r_rp.fr(c),
                        tag=f"{ph}.qr{s}{c}")
                    qbank.setf([a_r])
                    a_dmp = None
                    if debug == 2 and s == 0 and c == 0 and not is_b:
                        a_dmp = P.add("sync", lambda eng: eng.dma_start(
                            out=dbg_out["dmn"], in_=ep[:, 0, :]),
                            waits=[a_m], dma=True, dsem="dDmp", tag=f"{ph}.dmn")
                        a_dmp = P.add("sync", lambda eng: eng.dma_start(
                            out=dbg_out["drp"], in_=rp[:, 0, :]),
                            waits=[a_r, a_dmp], dma=True, dsem="dDmp", tag=f"{ph}.drp")
                    a_e = P.add("scalar", lambda eng, _c=c: eng.activation(
                        out=ep[:, _c % 2, :], in_=ep[:, _c % 2, :], func=AF.Exp),
                        waits=[a_m, a_dmp], tag=f"{ph}.qe{s}{c}")
                    if debug == 2 and s == 0 and c == 0 and not is_b:
                        a_dmp = P.add("sync", lambda eng: eng.dma_start(
                            out=dbg_out["dep"], in_=ep[:, 0, :]),
                            waits=[a_e], dma=True, dsem="dDmp", tag=f"{ph}.dep")
                    a_qf = P.add("gpsimd", lambda eng, _c=c: eng.tensor_tensor(
                        out=qfT[:, _c, :], in0=rp[:, _c % 2, :], in1=ep[:, _c % 2, :],
                        op=ALU.add), waits=[a_e, a_r, prev_omm, a_dmp],
                        tag=f"{ph}.qf{s}{c}")
                    if debug == 2 and s == 0 and c == 0 and not is_b:
                        a_qf = P.add("sync", lambda eng: eng.dma_start(
                            out=dbg_out["dq0"], in_=qfT[:, 0, :]),
                            waits=[a_qf], dma=True, dsem="dDq", tag=f"{ph}.dq0")
                    r_rp.setf(c, a_qf)
                    r_ep.setf(c, a_qf)
                art_qf[c] = a_qf
                if debug == 2 and s == 0 and not is_b and c in (1, 3, 5, 7):
                    kk = {1: "dqA", 3: "dqB", 5: "dqC", 7: "dqD"}[c]
                    art_qf[c] = P.add("sync", lambda eng, _k=kk: eng.dma_start(
                        out=dbg_out[_k], in_=qfT[:, 0, :]),
                        waits=[a_qf], dma=True, dsem="dDq", tag=f"{ph}.{kk}")
            if not is_b:
                for c in range(8):
                    gbank = gbanks[c % 2]
                    a_gmm = P.add("tensor", mm_group(
                        gbank.ap[:, :],
                        lambda k, _c=c: w_sl[3][:, k, _c * 128:(_c + 1) * 128],
                        lambda k: xlnt[:, k, :]),
                        waits=wready["wg_a"] + gbank.free, tag=f"{ph}.gmm{s}{c}")
                    qg_arts.append(a_gmm)
                    a_g = P.add("scalar", lambda eng, _c=c, _b=gbank: eng.activation(
                        out=gateC[:, _c, :], in_=_b.ap[:, :], func=AF.Sigmoid,
                        bias=auxfS[:, 40 + _c:40 + _c + 1]),
                        waits=[a_gmm, art_aux_f] + r_gate.fr(c), tag=f"{ph}.g{s}{c}")
                    gbank.setf(a_g)
                    art_ga[c] = a_g
            a_dump_qf = None
            if is_b:
                dbank = dbanks[s % 2]
                a_dmm = P.add("tensor", mm_group(
                    dbank.ap[:, :],
                    lambda k: t1[:, k * 128:(k + 1) * 128],
                    lambda k: qfT[:, k, :]),
                    waits=[art_qf[7], kvbd_ready] + dbank.free, tag=f"{ph}.dmm{s}")
                a_ds = P.add("vector", lambda eng, _b=dbank: eng.tensor_tensor(
                    out=dens[0:16, :], in0=_b.ap[0:16, :], in1=idecS[:, :], op=ALU.add),
                    waits=[a_dmm, a_id], tag=f"{ph}.ds{s}")
                dbank.setf(a_ds)
                r_idec.setf(0, a_ds)
                a_rec = P.add("vector", lambda eng: eng.reciprocal(
                    out=dens[0:16, :], in_=dens[0:16, :]), waits=[a_ds],
                    tag=f"{ph}.rec{s}")
                a_rec = P.add("vector", lambda eng: eng.tensor_copy(
                    out=rec16[0:16, :], in_=dens[0:16, :]), waits=[a_rec],
                    tag=f"{ph}.rec16{s}")
            for c in range(8):
                if is_b:
                    rbank = rbanks[c % 2]
                    a_rbm = P.add("tensor", lambda eng, _c=c, _b=rbank: eng.matmul(
                        _b.ap[:, :], auxbS[:, 256 + _c * 128:256 + (_c + 1) * 128],
                        rec16[:, :], start=True, stop=True),
                        waits=[a_rec, art_aux_b, art_r16z] + rbank.free,
                        tag=f"{ph}.rbm{s}{c}")
                    a_rbe = P.add("vector", lambda eng, _c=c, _b=rbank: eng.tensor_copy(
                        out=recB[:, _c % 2, :], in_=_b.ap[:, :]),
                        waits=[a_rbm] + r_recB.fr(c), tag=f"{ph}.rbe{s}{c}")
                    rbank.setf(a_rbe)
                abank = abanks[c % 2]
                a_amm = P.add("tensor", lambda eng, _c=c, _b=abank: eng.matmul(
                    _b.ap[:, :], kvBD[:, _c * 128:(_c + 1) * 128], qfT[:, _c, :],
                    start=True, stop=True),
                    waits=[art_qf[c], kvbd_ready] + abank.free,
                    tag=f"{ph}.amm{s}{c}")
                if is_b:
                    a_ga = P.add("vector", lambda eng, _c=c, _b=abank: eng.tensor_tensor(
                        out=qfT[:, _c, :], in0=_b.ap[:, :], in1=recB[:, _c % 2, :],
                        op=ALU.mult), waits=[a_amm, a_rbe], tag=f"{ph}.ga{s}{c}")
                    r_recB.setf(c, a_ga)
                else:
                    a_ga = P.add("vector", lambda eng, _c=c, _b=abank: eng.tensor_tensor(
                        out=gateC[:, _c, :], in0=gateC[:, _c, :], in1=_b.ap[:, :],
                        op=ALU.mult), waits=[a_amm, art_ga[c]], tag=f"{ph}.ga{s}{c}")
                abank.setf(a_ga)
                art_ga[c] = a_ga
            if s + 1 < NS:
                feA_next = [frontend_a(ph, 4 * (s + 1) + u, alpha=False)
                            for u in range(2)]
            a_dump_ga = None
            for u in range(4):
                i = 4 * s + u
                for h in range(2):
                    obank = obanks[(2 * u + h) % 2]
                    ga_src = qfT if is_b else gateC
                    a_omm = P.add("tensor", mm_group(
                        obank.ap[:, :],
                        lambda k, _u=u, _g=ga_src: _g[:, k, _u * 128:(_u + 1) * 128],
                        lambda k, _h=h, _o=oslot: w_sl[_o][:, k, _h * 512:(_h + 1) * 512],
                        biasrow=boB[:, h * 512:(h + 1) * 512]),
                        waits=[art_ga[7], a_dump_ga, boB_rdy, art_aux_b]
                              + wready[oname] + obank.free,
                        tag=f"{ph}.omm{s}{u}{h}")
                    prev_omm = a_omm
                    a_x = P.add("vector", lambda eng, _i=i, _h=h, _b=obank:
                        eng.tensor_tensor(
                            out=X[:, _i, _h * 512:(_h + 1) * 512],
                            in0=X[:, _i, _h * 512:(_h + 1) * 512], in1=_b.ap[:, :],
                            op=ALU.add),
                        waits=[a_omm], tag=f"{ph}.ox{s}{u}{h}")
                    obank.setf(a_x)
                    Xready[i] = a_x
            for u in range(4):
                r_xlnt.setf(u, qg_arts[-1])
            if not is_b:
                for c2 in range(8):
                    r_gate.setf(c2, prev_omm)
            if s + 1 < NS:
                b0 = frontend_b(ph, 4 * (s + 1), 0, feA_next[0])
                b1 = frontend_b(ph, 4 * (s + 1) + 1, 1, feA_next[1])
                a2 = frontend_a(ph, 4 * (s + 1) + 2, alpha=False)
                a3 = frontend_a(ph, 4 * (s + 1) + 3, alpha=False)
                b2 = frontend_b(ph, 4 * (s + 1) + 2, 2, a2)
                b3 = frontend_b(ph, 4 * (s + 1) + 3, 3, a3)
                fe_arts = [b0, b1, b2, b3]
            if side_work:
                side_work.pop(0)(s)
        while side_work:
            side_work.pop(0)(NS - 1)
        return

    def phase_ffn(ph, final, side_work, fe_pre=None):
        banks = [pb["A"], pb["B"], pb["C"]]
        pi = 0
        fe_art = [None] * (NT + 2)
        fe_a = [None] * (NT + 2)
        if fe_pre is not None:
            fe_art[0], fe_art[1] = fe_pre
        else:
            fe_art[0] = frontend(ph, 0, 0, alpha=True)
            if NT > 1:
                fe_art[1] = frontend(ph, 1, 1, alpha=True)
        for i in range(NT):
            if i + 2 < NT:
                fe_a[i + 2] = frontend_a(ph, i + 2, alpha=True)
            a_xlt = fe_art[i]
            last_arts = []
            a_o_last = None
            mm_arts = []
            evac_work = []
            for h in range(2):
                bank = banks[pi % 3]
                pi += 1
                a_mm = P.add("tensor", mm_group(
                    bank.ap[:, :],
                    lambda k, _i=i: xlnt[:, k, (_i % 4) * 128:((_i % 4) + 1) * 128],
                    lambda k, _h=h: w_sl[1][:, k, _h * 512:(_h + 1) * 512],
                    biasrow=bB["fb"][:, h * 512:(h + 1) * 512]),
                    waits=[a_xlt, bB["fb_ready"], art_aux_b] + wready["ternT"]
                          + bank.free,
                    tag=f"{ph}.fmm{i}{h}")
                last_arts.append(a_mm)
                mm_arts.append((h, bank, a_mm))
            r_xlnt.setf(i % 4, last_arts[-1])
            if i + 2 < NT:
                fe_art[i + 2] = frontend_b(ph, i + 2, (i + 2) % 4, fe_a[i + 2])
            for h, bank, a_mm in mm_arts:
                if final:
                    a_o = P.add("vector", lambda eng, _i=i, _h=h, _b=bank:
                        eng.tensor_tensor(
                            out=xland[:, _h * 512:(_h + 1) * 512],
                            in0=X[:, _i, _h * 512:(_h + 1) * 512], in1=_b.ap[:, :],
                            op=ALU.add),
                        waits=[a_mm] + r_xland.fr(i), tag=f"{ph}.fo{i}{h}")
                    bank.setf(a_o)
                    a_o_last = a_o
                else:
                    a_x = P.add("vector", lambda eng, _i=i, _h=h, _b=bank:
                        eng.tensor_tensor(
                            out=X[:, _i, _h * 512:(_h + 1) * 512],
                            in0=X[:, _i, _h * 512:(_h + 1) * 512], in1=_b.ap[:, :],
                            op=ALU.add),
                        waits=[a_mm], tag=f"{ph}.fx{i}{h}")
                    bank.setf(a_x)
                    Xready[i] = a_x
            if final:
                a_st = P.add("sync", lambda eng, _i=i: eng.dma_start(
                    out=y_out[_i * 128:(_i + 1) * 128, :], in_=xland[:, :]),
                    waits=[a_o_last], dma=True, dsem="dO", tag=f"{ph}.st{i}")
                r_xland.setf(i, a_st)
            if side_work:
                side_work.pop(0)(i)
        while side_work:
            side_work.pop(0)(NT - 1)
        return

    side_tasks = []
    for i in range(12, NT, 2):
        side_tasks.append(lambda i2, _i=i: load_x_pair(_i))
    for c in range(NCH):
        side_tasks.append(lambda i, _c=c: wload_plain("ph1", "wq_a", 2, _c))
    for c in range(NCH):
        side_tasks.append(lambda i, _c=c: wload_plain("ph1", "wg_a", 3, _c))

    def wmean_chunk(c):
        r = c % 2
        a_d = P.add("sync", lambda eng, _c=c, _r=r: eng.dma_start(
            out=wland[:, _r, :], in_=w_hbm["wffn"][_c * 128:(_c + 1) * 128, :]),
            waits=r_wland.fr(c), dma=True, dsem=f"dW{c % 2}", tag=f"ph1.md{c}")
        a_a = P.add("vector", lambda eng, _c=c, _r=r: eng.tensor_reduce(
            out=absacc[:, _c:_c + 1], in_=wland[:, _r, :], axis=mybir.AxisListType.X,
            op=ALU.add, apply_absolute_value=True),
            waits=[a_d], tag=f"ph1.ma{c}")
        r_wland.setf(c, a_a)
        wready.setdefault("wmean", [None] * 8)[c] = a_a
    for c in range(NCH):
        side_tasks.append(lambda i, _c=c: wmean_chunk(_c))
    side1 = []
    ntasks = len(side_tasks)
    pos = 0
    for t in range(NT):
        take = (ntasks * (t + 1)) // NT - (ntasks * t) // NT
        chunk = side_tasks[pos:pos + take]
        pos += take
        side1.append(lambda i, _ch=tuple(chunk): [f(i) for f in _ch])

    kvb_a = [pb["D"], pb["E"]]
    art_kvS_a, _ = phase_p1("ph1", False, 0, 1, kvb_a, side1,
                            proj=[pb["A"], pb["B"], pb["C"], pb["F"]])
    wfree[0] = []
    wfree[0] = [art_kvS_a]
    wfree[1] = [art_kvS_a]

    a_dm1 = P.add("tensor", lambda eng: eng.matmul(
        pb["F"].ap[0:1, 0:8], onesf_col, absacc[:, :], start=True, stop=True),
        waits=wready["wmean"] + [art_aux_f] + pb["F"].free, tag="ph1.dm1")
    a_dr = P.add("vector", lambda eng: eng.tensor_reduce(
        out=dsc[0:1, 0:1], in_=pb["F"].ap[0:1, 0:8], axis=mybir.AxisListType.X,
        op=ALU.add), waits=[a_dm1], tag="ph1.dr")
    a_dsc = P.add("vector", lambda eng: eng.tensor_scalar(
        out=dsc[0:1, 1:2], in0=dsc[0:1, 0:1], scalar1=THRESH / float(D * D),
        scalar2=None, op0=ALU.mult), waits=[a_dr], tag="ph1.dsc")
    a_dbm = P.add("tensor", lambda eng: eng.matmul(
        pb["F"].ap[:, 8:9], onesf_row, dsc[0:1, 1:2], start=True, stop=True),
        waits=[a_dsc], tag="ph1.dbm")
    a_db = P.add("vector", lambda eng: eng.tensor_copy(
        out=deltaB[:, :], in_=pb["F"].ap[:, 8:9]), waits=[a_dbm], tag="ph1.db")
    a_ndb = P.add("vector", lambda eng: eng.tensor_scalar(
        out=ndeltaB[:, :], in0=deltaB[:, :], scalar1=-1.0, scalar2=None,
        op0=ALU.mult), waits=[a_db], tag="ph1.ndb")
    pb["F"].setf(a_db)
    art_deltaB[0] = a_db
    art_deltaB[1] = a_ndb

    P.barrier()
    xa_cc, xa_cco = exchange_start("phxa", False, art_kvS_a, "cc1", cca_i, cca_o)
    for c in range(NCH):
        wload_plain("ph2", "wo_a", 0, c)
    fe_pre2 = [frontend("ph2", u, u, alpha=False) for u in range(4)]
    kvbd_a = exchange_finish("phxa", False, xa_cc, xa_cco, [])
    if debug:
        P.add("sync", lambda eng: eng.dma_start(out=dbg_out["dkv"], in_=kvS[:, :]),
              waits=[kvbd_a], dma=True, dsem="dDkv", tag="dbg.dkv")
        kvbd_a = P.add("sync", lambda eng: eng.dma_start(
            out=dbg_out["dkvbd"], in_=kvBD[:, :]), waits=[kvbd_a], dma=True,
            dsem="dDkv", tag="dbg.dkvbd")

    side2 = [lambda s, _c=c: tern_chunk("ph2", _c) for c in range(NCH)]
    while len(side2) < NS:
        side2.append(lambda s: None)
    P.barrier()
    phase_p2("ph2", False, 2, 0, kvbd_a, side2, fe_pre=fe_pre2)
    wfree[2] = [Xready[NT - 1]]
    wfree[3] = [Xready[NT - 1]]
    wfree[0] = [Xready[NT - 1]]

    def dump_X(key):
        if not debug:
            return
        for i in range(NT):
            P.add("gpsimd", lambda eng, _i=i, _k=key: eng.dma_start(
                out=dbg_out[_k][_i * 128:(_i + 1) * 128, :], in_=X[:, _i, :]),
                waits=[Xready[i]], dma=True, dsem="dDX", tag=f"dbg{key}{i}")

    dump_X("d1")

    side3 = []
    for c in range(NCH):
        side3.append(lambda i, _c=c: wload_transposed("ph3", "wk_b", 2, _c))
    for c in range(NCH):
        side3.append(lambda i, _c=c: wload_transposed("ph3", "wv_b", 3, _c))
    while len(side3) < NT:
        side3.append(lambda i: None)
    fe_pre3 = [frontend("ph3", 0, 0, alpha=True),
               frontend("ph3", 1, 1, alpha=True)]
    P.barrier()
    phase_ffn("ph3", False, side3, fe_pre=fe_pre3)

    dump_X("d2")

    side4 = [lambda i, _c=c: wload_transposed("ph4", "wq_b", 0, _c)
             for c in range(NCH)]
    while len(side4) < NT:
        side4.append(lambda i: None)
    fe_pre4 = [frontend("ph4", 0, 0, alpha=False),
               frontend("ph4", 1, 1, alpha=False)]
    P.barrier()
    kvb_b = [pb["D"], pb["E"], pb["F"]]
    art_kvS_b, _ = phase_p1("ph4", True, 2, 3, kvb_b, side4, fe_pre=fe_pre4)
    wfree[2] = [art_kvS_b]
    wfree[3] = [art_kvS_b]

    P.barrier()
    xb_cc, xb_cco = exchange_start("phxb", True, art_kvS_b, "cc2", ccb_i, ccb_o)
    for c in range(NCH):
        wload_transposed("ph5", "wo_b", 3, c)
    fe_pre5 = [frontend("ph5", u, u, alpha=False) for u in range(4)]
    kvbd_b = exchange_finish("phxb", True, xb_cc, xb_cco, [Xready[NT - 1]])

    P.barrier()
    phase_p2("ph5", True, 0, 3, kvbd_b, [], fe_pre=fe_pre5)

    dump_X("d3")

    fe_pre6 = [frontend("ph6", 0, 0, alpha=True),
               frontend("ph6", 1, 1, alpha=True)]
    P.barrier()
    phase_ffn("ph6", True, [], fe_pre=fe_pre6)

    fin_waits = [("dO", P.cnt.get("dO", 0))]
    if debug:
        fin_waits.append(("dDX", P.cnt.get("dDX", 0)))
    P.add("sync", lambda eng: eng.nop(),
          waits=[w for w in fin_waits if w[1] > 0], tag="fin")

    P.emit()
    es.close()
    return nc


def _prep_in_maps(inp, NT, nseq):
    T = NT * 128
    n_cores = 8
    x = np.ascontiguousarray(
        np.asarray(inp["L0_attn_a_x"], dtype=np.float32).reshape(-1, D))
    assert x.shape[0] == n_cores * T

    wmap = {
        "wq_a": inp["L0_attn_a_Wq"], "wk_a": inp["L0_attn_a_Wk"],
        "wv_a": inp["L0_attn_a_Wv"], "wg_a": inp["L0_attn_a_Wg"],
        "wo_a": inp["L0_attn_a_Wo"],
        "wq_b": inp["L0_attn_b_W_q"], "wk_b": inp["L0_attn_b_W_k"],
        "wv_b": inp["L0_attn_b_W_v"], "wo_b": inp["L0_attn_b_W_o"],
        "wffn": inp["L0_ffn_weight"],
    }
    wmap = {k: np.ascontiguousarray(np.asarray(v, dtype=np.float32))
            for k, v in wmap.items()}
    biasrows = np.ascontiguousarray(np.stack([
        np.asarray(inp["L0_attn_a_bk"], np.float32),
        np.asarray(inp["L0_attn_a_bv"], np.float32),
        np.asarray(inp["L0_attn_a_bo"], np.float32),
        np.asarray(inp["L0_attn_b_b_k"], np.float32),
        np.asarray(inp["L0_attn_b_b_v"], np.float32),
        np.asarray(inp["L0_attn_b_b_o"], np.float32),
        np.asarray(inp["L0_ffn_bias"], np.float32)]))
    bq_a = np.asarray(inp["L0_attn_a_bq"], np.float32)
    bg_a = np.asarray(inp["L0_attn_a_bg"], np.float32)
    bq_b = np.asarray(inp["L0_attn_b_b_q"], np.float32)
    alpha = np.asarray(inp["L0_ffn_alpha"], np.float32).reshape(-1)[0]

    sel = np.zeros((16, 1024), dtype=np.float32)
    for c in range(8):
        sel[2 * c, c * 128:c * 128 + 64] = 1.0
        sel[2 * c + 1, c * 128 + 64:(c + 1) * 128] = 1.0
    auxb = np.zeros((128, 1408), dtype=np.float32)
    auxb[:, 0:128] = np.eye(128, dtype=np.float32)
    auxb[:, 128:256] = 1.0
    auxb[0:16, 256:1280] = sel
    auxb[0, 1280:1408] = 1.0
    auxb = auxb.astype(ml_dtypes.bfloat16)

    in_maps = []
    for cid in range(n_cores):
        s0 = (cid % 2) * T
        n = s0 + np.arange(T, dtype=np.float64)
        dec = np.exp(-n / float(nseq)).astype(np.float32)
        idec = (EPS_B * np.exp(n / float(nseq))).astype(np.float32)
        auxf = np.zeros((128, 192), dtype=np.float32)
        auxf[:, 0:NT] = np.ascontiguousarray(dec.reshape(NT, 128).T)
        auxf[:, 32:40] = bq_a.reshape(8, 128).T
        auxf[:, 40:48] = bg_a.reshape(8, 128).T
        auxf[:, 48:56] = bq_b.reshape(8, 128).T
        auxf[:, 56] = alpha
        auxf[:, 57] = 1.0
        auxf[:, 58:186] = 1.0
        m = {"x": np.ascontiguousarray(x[cid * T:(cid + 1) * T]),
             "biasrows": biasrows,
             "auxf": np.ascontiguousarray(auxf),
             "auxb": np.ascontiguousarray(auxb),
             "idec16": np.ascontiguousarray(np.tile(idec, (16, 1)))}
        m.update(wmap)
        in_maps.append(m)
    return in_maps


def kernel(**inputs):
    NT = 32
    nseq = 8192
    in_maps = _prep_in_maps(inputs, NT, nseq)
    nc = build_nc(NT, nseq)
    res = run_bass_kernel_spmd(nc, in_maps, core_ids=list(range(8)))
    y = np.concatenate([np.asarray(res.results[c]["y"]) for c in range(8)], axis=0)
    return y.reshape(4, 8192, D).astype(np.float32)
